# version 09
# baseline (speedup 1.0000x reference)
import math
from contextlib import ExitStack
import numpy as np
import ml_dtypes
import concourse.bass as bass
import concourse.mybir as mybir
from concourse.bass_utils import run_bass_kernel_spmd

F32 = mybir.dt.float32
BF16 = mybir.dt.bfloat16
AF = mybir.ActivationFunctionType
ALU = mybir.AluOpType
AX = mybir.AxisListType

DEBUG = False

B, S, E, INNER, NH, DH, DHID = 8, 4096, 1024, 2048, 4, 512, 4096
TP1, TP3, SG = 512, 256, 512
NT1, NT3, NSG, NKT = S // TP1, S // TP3, S // SG, S // 128

C23 = float(1.5 * 2.0 ** 23)
S29 = float(2.0 ** -29)
CP = float(1.5 * 2.0 ** -6)
P29 = float(2.0 ** 29)
LNDH = float(-0.5 * math.log(512.0))


class Prog:
    ENGS = ("sync", "tensor", "vector", "scalar", "gpsimd")

    def __init__(self, nc, sems, cnt):
        self.nc = nc
        self.streams = {e: [] for e in self.ENGS}
        self.sems = sems
        self.cnt = cnt
        self.guards = {}

    def _app(self, eng, fn, inc, inc_n):
        if inc is not None:
            h = self.sems[inc]
            self.streams[eng].append(lambda e, fn=fn, h=h, n=inc_n: fn(e).then_inc(h, n))
            self.cnt[inc] += inc_n
            return self.cnt[inc]
        self.streams[eng].append(fn)
        return None

    def sy(self, fn, inc="ld", inc_n=16):
        return self._app("sync", fn, inc, inc_n)

    def t(self, fn, inc=None, inc_n=1):
        return self._app("tensor", fn, inc, inc_n)

    def v(self, fn, inc=None, inc_n=1):
        return self._app("vector", fn, inc, inc_n)

    def s(self, fn, inc=None, inc_n=1):
        return self._app("scalar", fn, inc, inc_n)

    def g(self, fn, inc="st", inc_n=16):
        if inc == "st":
            inc = "gst"
        return self._app("gpsimd", fn, inc, inc_n)

    def wait(self, eng, sem, value=None):
        v = self.cnt[sem] if value is None else value
        h = self.sems[sem]
        self.streams[eng].append(lambda e, h=h, v=v: e.wait_ge(h, v))

    def guard_set(self, key, sem):
        self.guards[key] = (sem, self.cnt[sem])

    def guard_wait(self, eng, key):
        if key in self.guards:
            sem, v = self.guards.pop(key)
            self.wait(eng, sem, v)

    def flush(self, block):
        for name in self.ENGS:
            thunks = self.streams[name]
            if not thunks:
                continue

            def runner(e, thunks=tuple(thunks)):
                for fn in thunks:
                    fn(e)

            getattr(block, name)(runner)
            self.streams[name] = []


def _blockdiag(w, transposed):
    out = np.zeros((16, 128, 128), np.float32)
    for h in range(512):
        c, hl = divmod(h, 32)
        blk = w[h] if transposed else np.transpose(w[h], (1, 0))
        out[c, hl * 4:hl * 4 + 4, hl * 4:hl * 4 + 4] = blk
    return out.reshape(2048, 128)


def prep_inputs(inp):
    f32 = np.float32
    x = np.asarray(inp["L0_attn_x"], f32)
    sh = {}
    sh["wup"] = np.ascontiguousarray(np.asarray(inp["L0_attn_W_up"], f32))
    sh["wdn"] = np.ascontiguousarray(np.asarray(inp["L0_attn_W_down"], f32))
    sh["w1T"] = np.ascontiguousarray(np.asarray(inp["L0_ffn_w1"], f32).T)
    sh["w2T"] = np.ascontiguousarray(np.asarray(inp["L0_ffn_w2"], f32).T)
    sh["wmod"] = np.ascontiguousarray(np.asarray(inp["L0_wmod"], f32))
    sh["wqbd"] = _blockdiag(np.asarray(inp["L0_attn_Wq"], f32), False)
    sh["wkbd"] = _blockdiag(np.asarray(inp["L0_attn_Wk"], f32), False)
    sh["wvbd"] = _blockdiag(np.asarray(inp["L0_attn_Wv"], f32), False)
    sh["wqbdT"] = _blockdiag(np.asarray(inp["L0_attn_Wq"], f32), True)
    sh["wkbdT"] = _blockdiag(np.asarray(inp["L0_attn_Wk"], f32), True)
    sh["wvbdT"] = _blockdiag(np.asarray(inp["L0_attn_Wv"], f32), True)
    sh["wig"] = np.ascontiguousarray(
        np.concatenate([np.asarray(inp["L0_attn_W_ig"], f32),
                        np.asarray(inp["L0_attn_W_fg"], f32)], axis=1))
    sh["bigfg"] = np.ascontiguousarray(np.concatenate(
        [np.asarray(inp["L0_attn_b_ig"], f32),
         np.asarray(inp["L0_attn_b_fg"], f32)]).reshape(8, 1))
    cw = np.asarray(inp["L0_attn_conv_w"], f32)
    c1 = np.zeros((128, 64), f32)
    for tap in range(4):
        for c in range(16):
            c1[:, tap * 16 + c] = cw[tap, c * 128:(c + 1) * 128]
    sh["conv1w"] = c1
    sh["conv1b"] = np.ascontiguousarray(np.asarray(inp["L0_attn_conv_b"], f32).reshape(16, 128).T)
    sh["normw"] = np.ascontiguousarray(np.asarray(inp["L0_attn_norm_w"], f32).reshape(16, 128).T)
    sh["skip"] = np.ascontiguousarray(np.asarray(inp["L0_attn_skip"], f32).reshape(16, 128).T)
    c2 = np.asarray(inp["L0_ffn_conv_w"], f32).reshape(4096, 3)
    cc2 = np.zeros((128, 96), f32)
    for tap in range(3):
        for c in range(32):
            cc2[:, tap * 32 + c] = c2[c * 128:(c + 1) * 128, tap]
    sh["conv2w"] = cc2
    sh["conv2b"] = np.ascontiguousarray(np.asarray(inp["L0_ffn_conv_b"], f32).reshape(32, 128).T)
    sh["onesrow"] = np.ones((1, 128), f32)
    sh["negonesrow"] = -np.ones((1, 128), f32)
    sh["onescol"] = np.ones((128, 4), f32)
    sh["ident"] = np.eye(128, dtype=f32)
    masks = np.zeros((128, 4 * SG), ml_dtypes.bfloat16)
    for r in range(4):
        ti = 128 * r + np.arange(128)[:, None]
        si = np.arange(SG)[None, :]
        masks[:, r * SG:(r + 1) * SG] = (ti <= si).astype(ml_dtypes.bfloat16)
    sh["masks"] = masks
    cmat = np.asarray(inp["L0_c"], f32)
    maps = []
    for b in range(B):
        m = dict(sh)
        m["xT"] = np.ascontiguousarray(x[b].T)
        m["crow"] = np.ascontiguousarray(cmat[b].reshape(8, 128).T)
        maps.append(m)
    return maps


def build(debug=False, phases=(0, 1, 2, 3)):
    nc = bass.Bass("TRN2", target_bir_lowering=False, detect_race_conditions=False)
    top = ExitStack()

    def dram(name, shape, dt=F32, kind="ExternalInput"):
        return nc.dram_tensor(name, list(shape), dt, kind=kind).ap()

    def sb(st, name, shape, dt=F32):
        return st.enter_context(nc.sbuf_tensor(name, list(shape), dt))

    def psum(st, name, shape, dt=F32):
        return st.enter_context(nc.psum_tensor(name, list(shape), dt))

    D = {}
    for nm, shp, dt in [
        ("xT", (E, S), F32), ("wup", (E, DHID), F32), ("wdn", (INNER, E), F32),
        ("w1T", (E, DHID), F32), ("w2T", (DHID, E), F32), ("wmod", (E, 6 * E), F32),
        ("wqbd", (INNER, 128), F32), ("wkbd", (INNER, 128), F32), ("wvbd", (INNER, 128), F32),
        ("wqbdT", (INNER, 128), F32), ("wkbdT", (INNER, 128), F32), ("wvbdT", (INNER, 128), F32),
        ("wig", (3 * INNER, 8), F32), ("bigfg", (8, 1), F32),
        ("conv1w", (128, 64), F32), ("conv1b", (128, 16), F32),
        ("normw", (128, 16), F32), ("skip", (128, 16), F32),
        ("conv2w", (128, 96), F32), ("conv2b", (128, 32), F32),
        ("onesrow", (1, 128), F32), ("negonesrow", (1, 128), F32),
        ("onescol", (128, 4), F32), ("ident", (128, 128), F32),
        ("masks", (128, 4 * SG), BF16), ("crow", (128, 8), F32),
    ]:
        D[nm] = dram("i_" + nm, shp, dt)
    SC = {}
    for nm, shp, dt in [
        ("modscr", (1, 6144), F32),
        ("q_s", (INNER, S), BF16), ("k_s", (INNER, S), BF16),
        ("ktok_s", (NH * S, DH), BF16), ("vtok_s", (NH * S, DH), BF16),
        ("xact_s", (INNER, S), BF16), ("sz_s", (INNER, S), BF16),
        ("hst_s", (INNER, S), BF16), ("x2_s", (E, S), F32),
        ("w1q_s", (E, DHID), BF16), ("w2q_s", (DHID, E), BF16),
        ("gates_s", (8, S), F32),
    ]:
        SC[nm] = dram("s_" + nm, shp, dt, kind="Internal")
    yT = dram("yT", (E, S), F32, kind="ExternalOutput")
    dbg = {}
    if debug:
        for nm, shp, dt in [("d_xact", (INNER, S), F32), ("d_q", (INNER, S), F32),
                            ("d_k", (INNER, S), F32), ("d_sz", (INNER, S), F32),
                            ("d_igfg", (8, S), F32), ("d_vtok", (NH * S, DH), F32),
                            ("d_mod", (128, 48), F32), ("d_G", (NH, S), F32),
                            ("d_M", (NH, S), F32), ("d_espm", (NH, S), F32),
                            ("d_w1q", (E, DHID), F32), ("d_w2q", (DHID, E), F32),
                            ("d_sw", (1, 4), F32), ("d_hst", (INNER, S), F32),
                            ("d_x2", (E, S), F32), ("d_omi", (128, 4), F32)]:
            dbg[nm] = dram(nm, shp, dt, kind="ExternalOutput")

    sems, cnt = {}, {}
    for n in ["ld", "gld", "mm", "dve", "act", "gp", "st", "gst", "ev", "aux", "fin"]:
        sems[n] = top.enter_context(nc.semaphore(n))
        cnt[n] = 0

    R = top
    onesrow = sb(R, "onesrow", (1, 128))
    negonesrow = sb(R, "negonesrow", (1, 128))
    onescol = sb(R, "onescol", (128, 4))
    thrcol = sb(R, "thrcol", (128, 2))
    onescol_b = sb(R, "onescol_b", (128, 2), BF16)
    ident = sb(R, "ident", (128, 128))
    conv1w = sb(R, "conv1w", (128, 64))
    conv1b = sb(R, "conv1b", (128, 16))
    normw = sb(R, "normw", (128, 16))
    skip_t = sb(R, "skip_t", (128, 16))
    conv2w = sb(R, "conv2w", (128, 96))
    conv2b = sb(R, "conv2b", (128, 32))
    modv = sb(R, "modv", (128, 48))
    omi = sb(R, "omi", (128, 4))
    swrow = sb(R, "swrow", (1, 4))
    weffa = sb(R, "weffa", (128, 16 * 8), BF16)
    weffv = sb(R, "weffv", (128, 16 * 8), BF16)
    weffa2 = sb(R, "weffa2", (128, 16 * 8), BF16)
    weffv2 = sb(R, "weffv2", (128, 16 * 8), BF16)
    bigfg_t = sb(R, "bigfg_t", (8, 2))
    atab = sb(R, "atab", (1, 64))
    mwcol = sb(R, "mwcol", (128, 2))
    identb = sb(R, "identb", (128, 128), BF16)
    g128 = sb(R, "g128", (128, 32 * NH))

    p = Prog(nc, sems, cnt)

    A = ExitStack()
    wupb = sb(A, "wupb", (128, 8 * DHID), BF16)
    qkw = sb(A, "qkw", (128, 2 * INNER), BF16)
    kvTw = sb(A, "kvTw", (128, 2 * INNER), BF16)
    igfg = sb(A, "igfg", (8, S))

    if 0 in phases:
        P0 = ExitStack()
        wslab = sb(P0, "wslab", (128, 4096))
        sgn = sb(P0, "sgn", (128, 2048))
        absw = sb(P0, "absw", (128, 2048))
        indw = sb(P0, "indw", (128, 2048), BF16)
        abssum = sb(P0, "abssum", (128, 44))
        siluc = sb(P0, "siluc", (128, 8))
        modrow = sb(P0, "modrow", (1, 512))
        mod48 = sb(P0, "mod48", (48, 128))
        wigsb = sb(P0, "wigsb", (128, 48 * 8))
        bdT = sb(P0, "bdT", (128, 3 * INNER))
        sgt = sb(P0, "sgt", (128, 8))
        sgt2 = sb(P0, "sgt2", (128, 8))
        ps_r = psum(P0, "ps_r", (128, 512))
        ps_s = psum(P0, "ps_s", (128, 512))
        ps_t = psum(P0, "ps_t", (128, 512))
        ps_q = psum(P0, "ps_q", (128, 512))
        ps_u = psum(P0, "ps_u", (128, 128))

        with nc.Block() as blk:
            for dst, src in [(onesrow, D["onesrow"]), (negonesrow, D["negonesrow"]),
                             (onescol, D["onescol"]), (ident, D["ident"]),
                             (conv1w, D["conv1w"]), (conv1b, D["conv1b"]), (normw, D["normw"]),
                             (skip_t, D["skip"]), (conv2w, D["conv2w"]), (conv2b, D["conv2b"]),
                             (siluc, D["crow"]), (bigfg_t, D["bigfg"])]:
                p.sy(lambda e, d=dst, s=src: e.dma_start(
                    out=d[:, 0:1] if d is bigfg_t else d[:], in_=s))
            p.sy(lambda e: e.dma_start(
                out=wigsb[:].rearrange("p (c n) -> p c n", c=48),
                in_=D["wig"].rearrange("(c p) n -> p c n", p=128)))
            for i, nm in enumerate(["wqbdT", "wkbdT", "wvbdT"]):
                p.sy(lambda e, i=i, nm=nm: e.dma_start(
                    out=bdT[:, i * INNER:(i + 1) * INNER].rearrange("p (c n) -> p c n", c=16),
                    in_=D[nm].rearrange("(c p) n -> p c n", p=128)))
            ld0 = cnt["ld"]
            p.g(lambda e: e.dma_start(out=onescol_b[:], in_=D["onescol"][:, 0:2]), inc="gld")
            p.g(lambda e: e.dma_start(out=identb[:], in_=D["ident"]), inc="gld")
            p.g(lambda e: e.dma_start(
                out=wupb[:].rearrange("p (c n) -> p c n", c=8),
                in_=D["wup"].rearrange("(c p) n -> p c n", p=128)), inc="gld")
            for i, nm in enumerate(["wqbd", "wkbd"]):
                p.g(lambda e, i=i, nm=nm: e.dma_start(
                    out=qkw[:, i * INNER:(i + 1) * INNER].rearrange("p (c n) -> p c n", c=16),
                    in_=D[nm].rearrange("(c p) n -> p c n", p=128)), inc="gld")
            for i, nm in enumerate(["wkbd", "wvbd"]):
                p.g(lambda e, i=i, nm=nm: e.dma_start(
                    out=kvTw[:, i * INNER:(i + 1) * INNER].rearrange("p (c n) -> p c n", c=16),
                    in_=D[nm].rearrange("(c p) n -> p c n", p=128)), inc="gld")

            p.wait("vector", "ld", ld0)
            p.v(lambda e: e.tensor_scalar(omi[:, 0:2], onescol[:, 0:2], S29, C23, ALU.mult, ALU.add), inc="dve")
            p.wait("vector", "dve")
            p.v(lambda e: e.tensor_scalar(omi[:, 2:4], omi[:, 0:2], C23, P29, ALU.subtract, ALU.mult), inc="dve")
            p.wait("vector", "dve")
            p.v(lambda e: e.tensor_sub(omi[:, 0:2], onescol[:, 0:2], omi[:, 2:4]), inc="dve")

            p.wait("scalar", "ld", ld0)
            p.s(lambda e: e.activation(siluc[:], siluc[:], AF.Silu), inc="act")
            act_siluc = cnt["act"]

            for g in range(12):
                p.guard_wait("sync", "wslab")
                p.sy(lambda e, g=g: e.dma_start(
                    out=wslab[:, 0:8 * 512].rearrange("p (c n) -> p c n", c=8),
                    in_=D["wmod"].rearrange("(c p) n -> p c n", p=128)[:, :, g * 512:(g + 1) * 512]))
                p.wait("tensor", "ld")
                if g == 0:
                    p.wait("tensor", "act", act_siluc)
                p.guard_wait("tensor", "psr_mod")
                for cc in range(8):
                    p.t(lambda e, cc=cc: e.matmul(
                        ps_r[0:1, 0:512], siluc[:, cc:cc + 1], wslab[:, cc * 512:(cc + 1) * 512],
                        start=(cc == 0), stop=(cc == 7)),
                        inc=("mm" if cc == 7 else None))
                p.guard_set("wslab", "mm")
                p.wait("vector", "mm")
                p.guard_wait("vector", "modrow")
                p.v(lambda e: e.tensor_copy(modrow[0:1, 0:512], ps_r[0:1, 0:512]), inc="dve")
                p.guard_set("psr_mod", "dve")
                p.wait("scalar", "dve")
                p.s(lambda e, g=g: e.dma_start(out=SC["modscr"][0:1, g * 512:(g + 1) * 512],
                                               in_=modrow[0:1, 0:512]), inc="st", inc_n=16)
                p.guard_set("modrow", "st")
            p.wait("sync", "st")
            p.sy(lambda e: e.dma_start(
                out=mod48[:], in_=SC["modscr"].rearrange("o (p n) -> (o p) n", p=48)))
            p.wait("tensor", "ld")
            p.t(lambda e: e.matmul(ps_u[:, 0:48], mod48[:], ident[0:48, 0:48], start=True, stop=True), inc="mm")
            p.wait("vector", "mm")
            p.v(lambda e: e.tensor_copy(modv[:], ps_u[:, 0:48]), inc="dve")
            p.wait("vector", "dve")
            p.v(lambda e: e.tensor_scalar_add(modv[:, 8:16], modv[:, 8:16], 1.0))
            p.v(lambda e: e.tensor_scalar_add(modv[:, 32:40], modv[:, 32:40], 1.0), inc="dve")
            p.wait("scalar", "dve")
            for base in (16, 40):
                p.s(lambda e, b=base: e.activation(sgt[:, 0:8], modv[:, b:b + 8], AF.Sign), inc="act")
                p.wait("scalar", "act")
                p.s(lambda e, b=base: e.activation(modv[:, b:b + 8], modv[:, b:b + 8], AF.Abs), inc="act")
                p.wait("vector", "act")
                p.v(lambda e, b=base: e.tensor_scalar_max(modv[:, b:b + 8], modv[:, b:b + 8], 0.25), inc="dve")
                p.wait("vector", "dve")
                p.v(lambda e, b=base: e.tensor_mul(modv[:, b:b + 8], modv[:, b:b + 8], sgt[:, 0:8]), inc="dve")
                p.wait("scalar", "dve")

            p.wait("tensor", "ld", ld0)
            p.wait("tensor", "dve", cnt["dve"])
            p.wait("tensor", "act", cnt["act"])
            for c in range(16):
                p.t(lambda e, c=c: e.matmul(ps_u[:, 0:8],
                                            bdT[:, c * 128:(c + 1) * 128],
                                            wigsb[:, c * 8:(c + 1) * 8], start=True, stop=False))
                p.t(lambda e, c=c: e.matmul(ps_u[:, 0:8], bdT[:, INNER + c * 128:INNER + (c + 1) * 128],
                                            wigsb[:, (16 + c) * 8:(17 + c) * 8], start=False, stop=True))
                p.t(lambda e, c=c: e.matmul(ps_u[:, 8:16], bdT[:, 2 * INNER + c * 128:2 * INNER + (c + 1) * 128],
                                            wigsb[:, (32 + c) * 8:(33 + c) * 8], start=True, stop=True), inc="mm")
                p.wait("vector", "mm")
                p.v(lambda e, c=c: e.tensor_copy(weffa[:, c * 8:(c + 1) * 8], ps_u[:, 0:8]))
                p.v(lambda e, c=c: e.tensor_copy(weffv[:, c * 8:(c + 1) * 8], ps_u[:, 8:16]), inc="dve")
                p.wait("vector", "dve")
                p.v(lambda e, c=c: e.tensor_copy(sgt[:, 0:8], weffa[:, c * 8:(c + 1) * 8]))
                p.v(lambda e, c=c: e.tensor_copy(sgt2[:, 0:8], weffv[:, c * 8:(c + 1) * 8]), inc="dve")
                p.wait("vector", "dve")
                p.v(lambda e, c=c: e.tensor_sub(sgt[:, 0:8], ps_u[:, 0:8], sgt[:, 0:8]))
                p.v(lambda e, c=c: e.tensor_sub(sgt2[:, 0:8], ps_u[:, 8:16], sgt2[:, 0:8]), inc="dve")
                p.wait("vector", "dve")
                p.v(lambda e, c=c: e.tensor_copy(weffa2[:, c * 8:(c + 1) * 8], sgt[:, 0:8]))
                p.v(lambda e, c=c: e.tensor_copy(weffv2[:, c * 8:(c + 1) * 8], sgt2[:, 0:8]), inc="dve")
                p.wait("tensor", "dve")

            subslabs = []
            co = 0
            for nm, nsl, cols, _ in [("w1T", 8, 4096, 0), ("w2T", 32, 1024, 8)]:
                for sl in range(nsl):
                    for c0_ in range(0, cols, 2048):
                        cw_ = min(2048, cols - c0_)
                        subslabs.append((nm, sl, c0_, cw_, co))
                    co += 1
            for nm, sl, c0_, cw_, co in subslabs:
                p.guard_wait("sync", "wslab")
                p.sy(lambda e, nm=nm, sl=sl, c0_=c0_, cw_=cw_: e.dma_start(
                    out=wslab[:, 0:cw_], in_=D[nm][sl * 128:(sl + 1) * 128, c0_:c0_ + cw_]))
                p.wait("vector", "ld")
                acc = (c0_ > 0)
                p.v(lambda e, co=co, cw_=cw_, acc=acc: e.reduce_sum(
                    abssum[:, co + (4 if acc else 0):co + (4 if acc else 0) + 1] if False else
                    abssum[:, co:co + 1] if not acc else abssum[:, 42:43],
                    wslab[:, 0:cw_], axis=AX.X, apply_absolute_value=True), inc="dve")
                if acc:
                    p.wait("vector", "dve")
                    p.v(lambda e, co=co: e.tensor_add(abssum[:, co:co + 1], abssum[:, co:co + 1],
                                                      abssum[:, 42:43]), inc="dve")
                p.guard_set("wslab", "dve")
            p.wait("vector", "dve")
            p.v(lambda e: e.reduce_sum(abssum[:, 40:41], abssum[:, 0:8], axis=AX.X))
            p.v(lambda e: e.reduce_sum(abssum[:, 41:42], abssum[:, 8:40], axis=AX.X), inc="dve")
            p.wait("tensor", "dve")
            p.t(lambda e: e.matmul(ps_u[0:1, 0:2], onescol[:, 0:1], abssum[:, 40:42],
                                   start=True, stop=True), inc="mm")
            p.wait("vector", "mm")
            p.v(lambda e: e.tensor_scalar(swrow[0:1, 0:2], ps_u[0:1, 0:2],
                                          1.0 / (DHID * E), 1e-5, ALU.mult, ALU.max), inc="dve")
            p.wait("vector", "dve")
            p.v(lambda e: e.tensor_scalar_mul(swrow[0:1, 2:4], swrow[0:1, 0:2], 0.5), inc="dve")
            p.wait("tensor", "dve")
            p.t(lambda e: e.matmul(ps_u[:, 0:2], onesrow[:], swrow[0:1, 2:4], start=True, stop=True), inc="mm")
            p.wait("vector", "mm")
            p.v(lambda e: e.tensor_copy(thrcol[:, 0:2], ps_u[:, 0:2]), inc="dve")
            p.wait("tensor", "dve")
            p.t(lambda e: e.matmul(ps_u[:, 0:2], onesrow[:], swrow[0:1, 0:2], start=True, stop=True), inc="mm")
            p.wait("vector", "mm")
            p.v(lambda e: e.tensor_copy(mwcol[:, 0:2], ps_u[:, 0:2]), inc="dve")
            p.wait("scalar", "dve")
            p.wait("vector", "dve")
            for nm, sl, c0_, cw_, co in subslabs:
                wi = 0 if nm == "w1T" else 1
                dst = SC["w1q_s"] if wi == 0 else SC["w2q_s"]
                p.guard_wait("sync", "wslab")
                p.sy(lambda e, nm=nm, sl=sl, c0_=c0_, cw_=cw_: e.dma_start(
                    out=wslab[:, 0:cw_], in_=D[nm][sl * 128:(sl + 1) * 128, c0_:c0_ + cw_]))
                ldh = cnt["ld"]
                p.wait("scalar", "ld", ldh)
                p.guard_wait("scalar", "sgnbuf")
                p.guard_wait("scalar", "abswbuf")
                p.s(lambda e, cw_=cw_: e.activation(sgn[:, 0:cw_], wslab[:, 0:cw_], AF.Sign))
                p.s(lambda e, cw_=cw_: e.activation(absw[:, 0:cw_], wslab[:, 0:cw_], AF.Abs), inc="act")
                p.guard_set("wslab", "act")
                p.wait("vector", "act")
                p.guard_wait("vector", "indbuf")
                p.v(lambda e, wi=wi, cw_=cw_: e.tensor_scalar(
                    indw[:, 0:cw_], absw[:, 0:cw_], thrcol[:, wi:wi + 1], None, ALU.is_gt), inc="dve")
                p.wait("vector", "dve")
                p.v(lambda e, cw_=cw_: e.tensor_mul(absw[:, 0:cw_], sgn[:, 0:cw_], indw[:, 0:cw_]), inc="dve")
                p.guard_set("sgnbuf", "dve")
                p.guard_set("indbuf", "dve")
                p.wait("gpsimd", "dve")
                p.g(lambda e, dst=dst, sl=sl, c0_=c0_, cw_=cw_: e.dma_start(
                    out=dst[sl * 128:(sl + 1) * 128, c0_:c0_ + cw_], in_=absw[:, 0:cw_]), inc="st")
                p.guard_set("abswbuf", "gst")
            if debug:
                p.wait("gpsimd", "dve")
                p.g(lambda e: e.dma_start(out=dbg["d_mod"], in_=modv[:]), inc="st")
                p.g(lambda e: e.dma_start(out=dbg["d_sw"], in_=swrow[:]), inc="st")
                p.g(lambda e: e.dma_start(out=dbg["d_omi"], in_=omi[:]), inc="st")
                for wi in range(2):
                    src_ = SC["w1q_s"] if wi == 0 else SC["w2q_s"]
                    dd = dbg["d_w1q"] if wi == 0 else dbg["d_w2q"]
                    p.wait("gpsimd", "gst")
                    p.g(lambda e, s_=src_, d_=dd: e.dma_start(out=d_, in_=s_), inc="st")
            p.wait("gpsimd", "gst")
            p.flush(blk)
        P0.close()

    if 1 in phases:
        P1 = ExitStack()
        xtb = sb(P1, "xtb", (128, 8 * TP1), BF16)
        sq1 = sb(P1, "sq1", (128, TP1))
        sq2 = sb(P1, "sq2", (128, TP1))
        rows1 = sb(P1, "rows1", (1, 3 * TP1))
        hb = sb(P1, "hb", (128, 8 * TP1), BF16)
        xm = sb(P1, "xm", (128, 16 * (TP1 + 3)), BF16)
        xact = sb(P1, "xact", (128, 16 * TP1), BF16)
        cacc = sb(P1, "cacc", (128, TP1))
        cacc2 = sb(P1, "cacc2", (128, TP1))
        szst = sb(P1, "szst", (128, 4 * TP1), BF16)
        qkst = sb(P1, "qkst", (128, 4 * TP1), BF16)
        tvst = sb(P1, "tvst", (128, 4 * DH), BF16)
        ps_a = psum(P1, "ps_a", (128, 512))
        ps_b = psum(P1, "ps_b", (128, 512))
        ps_c = psum(P1, "ps_c", (128, 512))
        ps_d = psum(P1, "ps_d", (128, 512))
        ps_rows = psum(P1, "ps_rows", (128, 512))
        ps_bc1 = psum(P1, "ps_bc1", (128, 512))
        ps_bc2 = psum(P1, "ps_bc2", (128, 512))
        ps_ig = psum(P1, "ps_ig", (8, 512))

        with nc.Block() as blk:
            p.wait("tensor", "gld")
            p.wait("tensor", "dve")
            for t in range(NT1):
                t0 = t * TP1
                p.guard_wait("gpsimd", "xtb")
                p.g(lambda e, t0=t0: e.dma_start(
                    out=xtb[:].rearrange("p (c n) -> p c n", c=8),
                    in_=D["xT"].rearrange("(c p) n -> p c n", p=128)[:, :, t0:t0 + TP1]), inc="gld")
                gld_x = cnt["gld"]
                p.wait("tensor", "gld", gld_x)
                for c in range(8):
                    p.t(lambda e, c=c: e.matmul(
                        ps_rows[0:1, 0:TP1], onescol_b[:, 0:1], xtb[:, c * TP1:(c + 1) * TP1],
                        start=(c == 0), stop=(c == 7)), inc=("mm" if c == 7 else None))
                p.wait("scalar", "gld", gld_x)
                for c in range(8):
                    d_ = sq1 if c % 2 == 0 else sq2
                    if c >= 2:
                        p.wait("scalar", "mm", cnt["mm"])
                    p.s(lambda e, c=c, d=d_: e.activation(d[:], xtb[:, c * TP1:(c + 1) * TP1], AF.Square), inc="act")
                    p.wait("tensor", "act")
                    p.t(lambda e, c=c, d=d_: e.matmul(
                        ps_rows[32:33, 0:TP1], onescol[:, 0:1], d[:],
                        start=(c == 0), stop=(c == 7)), inc="mm")
                p.wait("vector", "mm")
                s0 = rows1[0:1, 0:TP1]; s1 = rows1[0:1, TP1:2 * TP1]; s2 = rows1[0:1, 2 * TP1:3 * TP1]
                p.v(lambda e, s0=s0: e.tensor_scalar_mul(s0, ps_rows[0:1, 0:TP1], 1.0 / E))
                p.v(lambda e, s1=s1: e.tensor_scalar_mul(s1, ps_rows[32:33, 0:TP1], 1.0 / E), inc="dve")
                p.wait("vector", "dve")
                p.v(lambda e, s0=s0, s2=s2: e.tensor_mul(s2, s0, s0), inc="dve")
                p.wait("vector", "dve")
                p.v(lambda e, s1=s1, s2=s2: e.tensor_sub(s1, s1, s2), inc="dve")
                p.wait("vector", "dve")
                p.v(lambda e, s1=s1: e.tensor_scalar_add(s1, s1, 1e-5), inc="dve")
                p.wait("scalar", "dve")
                p.s(lambda e, s1=s1, s2=s2: e.activation(s2, s1, AF.Sqrt), inc="act")
                p.wait("vector", "act")
                p.v(lambda e, s1=s1, s2=s2: e.reciprocal(s1, s2), inc="dve")
                p.wait("vector", "dve")
                p.v(lambda e, s0=s0, s1=s1, s2=s2: e.tensor_mul(s2, s0, s1), inc="dve")
                p.wait("vector", "dve")
                p.v(lambda e, s2=s2: e.tensor_scalar_mul(s2, s2, -1.0), inc="dve")
                p.wait("tensor", "dve")
                p.t(lambda e, s1=s1: e.matmul(ps_bc1[:, 0:TP1], onesrow[:], s1, start=True, stop=True))
                p.t(lambda e, s2=s2: e.matmul(ps_bc2[:, 0:TP1], onesrow[:], s2, start=True, stop=True), inc="mm")
                p.wait("vector", "mm")
                p.guard_wait("vector", "hb")
                for c in range(8):
                    p.v(lambda e, c=c: e.tensor_mul(sq1[:], xtb[:, c * TP1:(c + 1) * TP1], ps_bc1[:, 0:TP1]))
                    p.v(lambda e, c=c: e.tensor_add(sq1[:], sq1[:], ps_bc2[:, 0:TP1]))
                    p.v(lambda e, c=c: e.tensor_scalar(
                        hb[:, c * TP1:(c + 1) * TP1], sq1[:], modv[:, 8 + c:9 + c], modv[:, 0 + c:1 + c],
                        ALU.mult, ALU.add), inc=("dve" if c == 7 else None))
                hb_done = cnt["dve"]
                p.wait("tensor", "dve", hb_done)
                for ic in range(32):
                    pb = ps_a if ic % 2 == 0 else ps_b
                    p.guard_wait("tensor", f"wubank{ic % 2}")
                    for kc in range(8):
                        p.t(lambda e, ic=ic, kc=kc, pb=pb: e.matmul(
                            pb[:, 0:TP1], wupb[:, kc * DHID + ic * 128:kc * DHID + (ic + 1) * 128],
                            hb[:, kc * TP1:(kc + 1) * TP1],
                            start=(kc == 0), stop=(kc == 7)), inc=("mm" if kc == 7 else None))
                    mmv = cnt["mm"]
                    if ic < 16:
                        p.wait("vector", "mm", mmv)
                        if t > 0 and ic == 0:
                            p.guard_wait("vector", "xmtile")
                        p.v(lambda e, ic=ic, pb=pb: e.tensor_copy(
                            xm[:, ic * (TP1 + 3) + 3:(ic + 1) * (TP1 + 3)], pb[:, 0:TP1]), inc="dve")
                        p.guard_set(f"wubank{ic % 2}", "dve")
                    else:
                        sl = ic % 4
                        p.wait("scalar", "mm", mmv)
                        if t > 0 or ic >= 20:
                            p.guard_wait("scalar", f"szst{sl}")
                        p.s(lambda e, sl=sl, pb=pb: e.activation(
                            szst[:, sl * TP1:(sl + 1) * TP1], pb[:, 0:TP1], AF.Silu), inc="act")
                        p.guard_set(f"wubank{ic % 2}", "act")
                        p.wait("gpsimd", "act")
                        p.g(lambda e, sl=sl, ic=ic, t0=t0: e.dma_start(
                            out=SC["sz_s"][(ic - 16) * 128:(ic - 15) * 128, t0:t0 + TP1],
                            in_=szst[:, sl * TP1:(sl + 1) * TP1]), inc="st")
                        p.guard_set(f"szst{sl}", "gst")
                if t == 0:
                    for c in range(16):
                        b = c * (TP1 + 3)
                        p.v(lambda e, b=b: e.tensor_scalar_mul(xm[:, b:b + 3], xm[:, b:b + 3] if False else onescol[:, 0:3], 0.0),
                            inc=("dve" if c == 15 else None))
                    p.wait("vector", "dve")
                p.guard_wait("scalar", "xact")
                p.guard_wait("scalar", "xactdma")
                for c in range(16):
                    b = c * (TP1 + 3)
                    cc_ = cacc[:, 0:TP1] if c % 2 == 0 else cacc2[:, 0:TP1]
                    p.guard_wait("vector", f"cacc{c % 2}")
                    p.v(lambda e, c=c, b=b, cc_=cc_: e.tensor_scalar(
                        cc_, xm[:, b:b + TP1], conv1w[:, 0 * 16 + c:0 * 16 + c + 1], None, ALU.mult))
                    for tp_ in range(1, 4):
                        p.v(lambda e, c=c, b=b, tp_=tp_, cc_=cc_: e.scalar_tensor_tensor(
                            cc_, xm[:, b + tp_:b + tp_ + TP1], conv1w[:, tp_ * 16 + c:tp_ * 16 + c + 1],
                            cc_, ALU.mult, ALU.add), inc=("dve" if tp_ == 3 else None))
                    p.wait("scalar", "dve")
                    p.s(lambda e, c=c, cc_=cc_: e.activation(
                        xact[:, c * TP1:(c + 1) * TP1], cc_, AF.Silu,
                        bias=conv1b[:, c:c + 1], scale=1.0), inc="act")
                    p.guard_set(f"cacc{c % 2}", "act")
                xact_done = cnt["act"]
                if t < NT1 - 1:
                    p.wait("vector", "act", xact_done)
                    for c in range(16):
                        b = c * (TP1 + 3)
                        p.v(lambda e, b=b: e.tensor_copy(xm[:, b:b + 3], xm[:, b + TP1:b + TP1 + 3]),
                            inc=("dve" if c == 15 else None))
                    p.guard_set("xmtile", "dve")
                p.wait("gpsimd", "act", xact_done)
                p.g(lambda e, t0=t0: e.dma_start(
                    out=SC["xact_s"].rearrange("(c p) n -> p c n", p=128)[:, :, t0:t0 + TP1],
                    in_=xact[:].rearrange("p (c n) -> p c n", c=16)), inc="st")
                
                p.wait("tensor", "act", xact_done)
                for c in range(16):
                    for iw, dst_s in ((0, SC["q_s"]), (1, SC["k_s"])):
                        pb = ps_c if (2 * c + iw) % 2 == 0 else ps_d
                        sl = (2 * c + iw) % 4
                        p.guard_wait("tensor", f"qkbank{(2 * c + iw) % 2}")
                        p.t(lambda e, c=c, iw=iw, pb=pb: e.matmul(
                            pb[:, 0:TP1], qkw[:, iw * INNER + c * 128:iw * INNER + (c + 1) * 128],
                            xact[:, c * TP1:(c + 1) * TP1], start=True, stop=True), inc="mm")
                        p.wait("vector", "mm")
                        p.guard_wait("vector", f"qkst{sl}")
                        p.v(lambda e, sl=sl, pb=pb: e.tensor_copy(qkst[:, sl * TP1:(sl + 1) * TP1], pb[:, 0:TP1]), inc="dve")
                        p.guard_set(f"qkbank{(2 * c + iw) % 2}", "dve")
                        p.wait("gpsimd", "dve")
                        p.g(lambda e, sl=sl, c=c, t0=t0, dst_s=dst_s: e.dma_start(
                            out=dst_s[c * 128:(c + 1) * 128, t0:t0 + TP1],
                            in_=qkst[:, sl * TP1:(sl + 1) * TP1]), inc="st")
                        p.guard_set(f"qkst{sl}", "gst")
                for ts_ in range(TP1 // 128):
                    for hd in range(NH):
                        for iw, src_t, dst_s in ((0, xact, SC["ktok_s"]), (1, xm, SC["vtok_s"])):
                            pb = ps_c if iw == 0 else ps_d
                            sl = (ts_ * NH + hd) % 2 * 2 + iw
                            p.guard_wait("tensor", f"qkbank{iw}")
                            for fc in range(4):
                                c = hd * 4 + fc
                                p.t(lambda e, c=c, ts_=ts_, fc=fc, iw=iw, pb=pb: e.matmul(
                                    pb[:, fc * 128:(fc + 1) * 128],
                                    (xact[:, c * TP1 + ts_ * 128:c * TP1 + (ts_ + 1) * 128] if iw == 0
                                     else xm[:, c * (TP1 + 3) + 3 + ts_ * 128:c * (TP1 + 3) + 3 + (ts_ + 1) * 128]),
                                    kvTw[:, iw * INNER + c * 128:iw * INNER + (c + 1) * 128],
                                    start=True, stop=True), inc=("mm" if fc == 3 else None))
                            p.wait("vector", "mm")
                            p.guard_wait("vector", f"tvst{sl}")
                            p.v(lambda e, sl=sl, pb=pb: e.tensor_copy(tvst[:, sl * DH:(sl + 1) * DH], pb[:, 0:DH]), inc="dve")
                            p.guard_set(f"qkbank{iw}", "dve")
                            p.wait("gpsimd", "dve")
                            p.g(lambda e, sl=sl, hd=hd, t0=t0, ts_=ts_, dst_s=dst_s: e.dma_start(
                                out=dst_s[hd * S + t0 + ts_ * 128:hd * S + t0 + (ts_ + 1) * 128, :],
                                in_=tvst[:, sl * DH:(sl + 1) * DH]), inc="st")
                            p.guard_set(f"tvst{sl}", "gst")
                p.guard_wait("tensor", "igps")
                for c in range(16):
                    p.t(lambda e, c=c: e.matmul(
                        ps_ig[0:8, 0:TP1], weffa[:, c * 8:(c + 1) * 8], xact[:, c * TP1:(c + 1) * TP1],
                        start=(c == 0), stop=False))
                    p.t(lambda e, c=c: e.matmul(
                        ps_ig[0:8, 0:TP1], weffa2[:, c * 8:(c + 1) * 8], xact[:, c * TP1:(c + 1) * TP1],
                        start=False, stop=False))
                for c in range(16):
                    p.t(lambda e, c=c: e.matmul(
                        ps_ig[0:8, 0:TP1], weffv[:, c * 8:(c + 1) * 8],
                        xm[:, c * (TP1 + 3) + 3:(c + 1) * (TP1 + 3)],
                        start=False, stop=False))
                    p.t(lambda e, c=c: e.matmul(
                        ps_ig[0:8, 0:TP1], weffv2[:, c * 8:(c + 1) * 8],
                        xm[:, c * (TP1 + 3) + 3:(c + 1) * (TP1 + 3)],
                        start=False, stop=(c == 15)), inc=("mm" if c == 15 else None))
                p.wait("vector", "mm")
                p.v(lambda e, t0=t0: e.tensor_scalar(
                    igfg[:, t0:t0 + TP1], ps_ig[0:8, 0:TP1], bigfg_t[0:8, 0:1], None, ALU.add), inc="dve")
                p.guard_set("igps", "dve")
                p.guard_set("xtb", "dve")
                p.guard_set("hb", "mm")
                p.guard_set("xact", "mm")
            if debug:
                p.wait("gpsimd", "dve")
                p.wait("gpsimd", "gst")
                p.g(lambda e: e.dma_start(out=dbg["d_igfg"], in_=igfg[:]), inc="st")
                p.g(lambda e: e.dma_start(out=dbg["d_xact"], in_=SC["xact_s"]), inc="st")
                p.g(lambda e: e.dma_start(out=dbg["d_q"], in_=SC["q_s"]), inc="st")
                p.g(lambda e: e.dma_start(out=dbg["d_k"], in_=SC["k_s"]), inc="st")
                p.g(lambda e: e.dma_start(out=dbg["d_sz"], in_=SC["sz_s"]), inc="st")
                p.g(lambda e: e.dma_start(out=dbg["d_vtok"], in_=SC["vtok_s"]), inc="st")
                p.wait("gpsimd", "gst")
            p.flush(blk)
        P1.close()

    if 1 in phases:
        P1b = ExitStack()
        fg4 = sb(P1b, "fg4", (NH, S))
        mrow = sb(P1b, "mrow", (NH, S))
        espm = sb(P1b, "espm", (NH, S))
        sprow = sb(P1b, "sprow", (NH, S))
        sprow2 = sb(P1b, "sprow2", (NH, S))
        grow_t = sb(P1b, "grow_t", (NH, S))
        g32 = sb(P1b, "g32", (32, 128))
        ps_g = psum(P1b, "ps_g", (128, 512))
        with nc.Block() as blk:
            p.wait("sync", "dve")
            p.sy(lambda e: e.dma_start(out=fg4[:], in_=igfg[4:8, :]))
            p.wait("scalar", "ld")
            p.s(lambda e: e.activation(sprow[:], fg4[:], AF.Exp, scale=-1.0), inc="act")
            p.wait("scalar", "act")
            p.s(lambda e: e.activation(sprow[:], sprow[:], AF.Ln, bias=1.0), inc="act")
            p.wait("vector", "act")
            p.v(lambda e: e.tensor_tensor_scan(sprow2[:], sprow[:], sprow[:], 0.0, ALU.add, ALU.bypass), inc="dve")
            p.wait("vector", "dve")
            p.v(lambda e: e.tensor_add(grow_t[:], igfg[0:4, :], sprow2[:]), inc="dve")
            p.wait("vector", "dve")
            p.v(lambda e: e.tensor_tensor_scan(mrow[:], grow_t[:], grow_t[:], -1e30, ALU.max, ALU.bypass), inc="dve")
            p.wait("vector", "dve")
            p.v(lambda e: e.tensor_sub(espm[:], sprow2[:], mrow[:]), inc="dve")
            p.wait("scalar", "dve")
            p.s(lambda e: e.activation(espm[:], espm[:], AF.Exp), inc="act")
            p.wait("vector", "dve")
            p.v(lambda e: e.tensor_scalar_add(grow_t[:], grow_t[:], LNDH), inc="dve")
            p.wait("scalar", "dve")
            if debug:
                p.wait("gpsimd", "act")
                p.g(lambda e: e.dma_start(out=dbg["d_G"], in_=grow_t[:]), inc="st")
                p.g(lambda e: e.dma_start(out=dbg["d_M"], in_=mrow[:]), inc="st")
                p.g(lambda e: e.dma_start(out=dbg["d_espm"], in_=espm[:]), inc="st")
            for hd in range(NH):
                p.guard_wait("scalar", "modscr")
                p.s(lambda e, hd=hd: e.dma_start(out=SC["modscr"][0:1, 0:4096], in_=grow_t[hd:hd + 1, :]),
                    inc="st", inc_n=16)
                p.wait("sync", "st")
                p.guard_wait("sync", "g32")
                p.sy(lambda e: e.dma_start(
                    out=g32[:], in_=SC["modscr"][0:1, 0:4096].rearrange("o (p n) -> (o p) n", p=32)))
                p.guard_set("modscr", "ld")
                p.wait("tensor", "ld")
                if hd == 0:
                    p.wait("tensor", "act")
                p.t(lambda e: e.matmul(ps_g[:, 0:32], g32[:], ident[0:32, 0:32], start=True, stop=True), inc="mm")
                p.guard_set("g32", "mm")
                p.wait("vector", "mm")
                p.v(lambda e, hd=hd: e.tensor_copy(g128[:, hd * 32:(hd + 1) * 32], ps_g[:, 0:32]), inc="dve")
                p.wait("tensor", "dve")
            p.wait("gpsimd", "dve")
            p.wait("gpsimd", "act")
            p.g(lambda e: e.dma_start(out=SC["gates_s"][0:4, :], in_=mrow[:]), inc="st")
            p.g(lambda e: e.dma_start(out=SC["gates_s"][4:8, :], in_=espm[:]), inc="st")
            for hd in range(NH):
                p.g(lambda e, hd=hd: e.dma_start(
                    out=atab[0:1, hd * 8:hd * 8 + 8],
                    in_=mrow[hd:hd + 1, :].rearrange("o (i r) -> o i r", r=SG)[:, :, 0:1],
                    allow_slow_non_contiguous=True), inc="st")
            p.wait("gpsimd", "gst")
            p.flush(blk)
        P1b.close()
    A.close()

    if 2 in phases:
        P2 = ExitStack()
        masksb = sb(P2, "masksb", (128, 4 * SG), BF16)
        qbuf = sb(P2, "qbuf", (128, 4 * S), BF16)
        kbuf = sb(P2, "kbuf", (128, 4 * S), BF16)
        vbuf = sb(P2, "vbuf", (128, NKT * 512), BF16)
        xabuf = sb(P2, "xabuf", (128, 4 * SG), BF16)
        szbuf = sb(P2, "szbuf", (128, 4 * SG), BF16)
        st1 = sb(P2, "st1", (128, SG), BF16)
        st2 = sb(P2, "st2", (128, SG), BF16)
        ei = sb(P2, "ei", (128, 32))
        htr = sb(P2, "htr", (128, 4 * SG), BF16)
        sqh = sb(P2, "sqh", (128, SG))
        sqh2 = sb(P2, "sqh2", (128, SG))
        nfb = sb(P2, "nfb", (128, SG))
        mub = sb(P2, "mub", (128, SG), BF16)
        r1b = sb(P2, "r1b", (128, SG), BF16)
        rowsp = sb(P2, "rowsp", (1, 4 * SG))
        mslice = sb(P2, "mslice", (1, SG))
        nega = sb(P2, "nega", (128, 2))
        eslice = sb(P2, "eslice", (1, SG))
        hstg = sb(P2, "hstg", (128, 4 * SG), BF16)
        ps_S0 = psum(P2, "ps_S0", (128, 512))
        ps_S1 = psum(P2, "ps_S1", (128, 512))
        ps_h0 = psum(P2, "ps_h0", (128, 512))
        ps_h1 = psum(P2, "ps_h1", (128, 512))
        ps_h2 = psum(P2, "ps_h2", (128, 512))
        ps_h3 = psum(P2, "ps_h3", (128, 512))
        ps_rs = psum(P2, "ps_rs", (128, 512))
        ps_bc = psum(P2, "ps_bc", (128, 512))
        HPS = [ps_h0, ps_h1, ps_h2, ps_h3]

        with nc.Block() as blk:
            p.wait("sync", "gst", cnt["gst"])
            p.wait("tensor", "gst", cnt["gst"])
            p.sy(lambda e: e.dma_start(out=masksb[:], in_=D["masks"]))
            for hd in range(NH):
                p.guard_wait("sync", "p2bufs")
                p.sy(lambda e, hd=hd: e.dma_start(
                    out=qbuf[:].rearrange("p (c n) -> p c n", c=4),
                    in_=SC["q_s"].rearrange("(c p) n -> p c n", p=128)[:, hd * 4:hd * 4 + 4, :]))
                p.sy(lambda e, hd=hd: e.dma_start(
                    out=kbuf[:].rearrange("p (c n) -> p c n", c=4),
                    in_=SC["k_s"].rearrange("(c p) n -> p c n", p=128)[:, hd * 4:hd * 4 + 4, :]))
                p.sy(lambda e, hd=hd: e.dma_start(
                    out=vbuf[:].rearrange("p (c n) -> p c n", c=NKT),
                    in_=SC["vtok_s"].rearrange("(c p) n -> p c n", p=128)[:, hd * NKT:(hd + 1) * NKT, :]))
                ld_head = cnt["ld"]
                for i in range(NSG):
                    s0 = i * SG
                    njt = 4 * i + 4
                    p.wait("tensor", "ld", ld_head)
                    p.guard_wait("tensor", "psbc")
                    p.t(lambda e, hd=hd, i=i: e.matmul(
                        ps_bc[:, 0:1], negonesrow[:], atab[0:1, hd * 8 + i:hd * 8 + i + 1],
                        start=True, stop=True), inc="mm")
                    p.wait("vector", "mm")
                    p.guard_wait("vector", "negabuf")
                    p.v(lambda e: e.tensor_copy(nega[:, 0:1], ps_bc[:, 0:1]), inc="dve")
                    p.guard_set("psbc", "dve")
                    p.wait("scalar", "dve")
                    p.guard_wait("scalar", "eibuf")
                    p.s(lambda e, hd=hd, njt=njt: e.activation(
                        ei[:, 0:njt], g128[:, hd * 32:hd * 32 + njt], AF.Exp,
                        bias=nega[:, 0:1], scale=1.0), inc="act")
                    p.guard_set("negabuf", "act")
                    ei_ready = cnt["act"]
                    p.guard_wait("sync", "gsl")
                    p.sy(lambda e, hd=hd, s0=s0: e.dma_start(
                        out=mslice[:], in_=SC["gates_s"][hd:hd + 1, s0:s0 + SG]))
                    p.sy(lambda e, hd=hd, s0=s0: e.dma_start(
                        out=eslice[:], in_=SC["gates_s"][4 + hd:5 + hd, s0:s0 + SG]))
                    p.guard_wait("sync", "xaszbuf")
                    p.sy(lambda e, hd=hd, s0=s0: e.dma_start(
                        out=xabuf[:].rearrange("p (c n) -> p c n", c=4),
                        in_=SC["xact_s"].rearrange("(c p) n -> p c n", p=128)[:, hd * 4:hd * 4 + 4, s0:s0 + SG]))
                    p.sy(lambda e, hd=hd, s0=s0: e.dma_start(
                        out=szbuf[:].rearrange("p (c n) -> p c n", c=4),
                        in_=SC["sz_s"].rearrange("(c p) n -> p c n", p=128)[:, hd * 4:hd * 4 + 4, s0:s0 + SG]))
                    ld_sg = cnt["ld"]
                    p.guard_wait("tensor", "hps")
                    p.guard_wait("tensor", "psrs")
                    for j in range(njt):
                        sps = ps_S0 if j % 2 == 0 else ps_S1
                        sbb = st1 if j % 2 == 0 else st2
                        p.guard_wait("tensor", f"sps{j % 2}")
                        for dc in range(4):
                            p.t(lambda e, hd=hd, i=i, j=j, dc=dc, sps=sps: e.matmul(
                                sps[:, 0:SG], kbuf[:, dc * S + j * 128:dc * S + (j + 1) * 128],
                                qbuf[:, dc * S + i * SG:dc * S + (i + 1) * SG],
                                start=(dc == 0), stop=(dc == 3)), inc=("mm" if dc == 3 else None))
                        mm_s = cnt["mm"]
                        p.wait("vector", "mm", mm_s)
                        if i == 0 and j == 0:
                            p.wait("vector", "act", ei_ready)
                        p.guard_wait("vector", f"sbb{j % 2}")
                        if j >= 4 * i:
                            r = j - 4 * i
                            p.v(lambda e, j=j, r=r, sps=sps, sbb=sbb: e.scalar_tensor_tensor(
                                sbb[:], sps[:, 0:SG], ei[:, j:j + 1], masksb[:, r * SG:(r + 1) * SG],
                                ALU.mult, ALU.mult), inc="dve")
                        else:
                            p.v(lambda e, j=j, sps=sps, sbb=sbb: e.tensor_scalar(
                                sbb[:], sps[:, 0:SG], ei[:, j:j + 1], None, ALU.mult), inc="dve")
                        ev_s = cnt["dve"]
                        p.guard_set(f"sps{j % 2}", "dve")
                        p.wait("tensor", "dve", ev_s)
                        for dc in range(4):
                            p.t(lambda e, j=j, dc=dc, sbb=sbb, njt=njt: e.matmul(
                                HPS[dc][:, 0:SG], vbuf[:, j * 512 + dc * 128:j * 512 + (dc + 1) * 128],
                                sbb[:], start=(j == 0), stop=(j == njt - 1)))
                        p.t(lambda e, j=j, sbb=sbb, njt=njt: e.matmul(
                            ps_rs[0:1, 0:SG], onescol_b[:, 0:1], sbb[:],
                            start=(j == 0), stop=(j == njt - 1)), inc=("mm" if j == njt - 1 else None))
                        p.guard_set(f"sbb{j % 2}", "mm")
                    mm_head = cnt["mm"]
                    ro0 = rowsp[0:1, 0:SG]
                    ro1 = rowsp[0:1, SG:2 * SG]
                    ro2 = rowsp[0:1, 2 * SG:3 * SG]
                    ro3 = rowsp[0:1, 3 * SG:4 * SG]
                    p.wait("scalar", "mm", mm_head)
                    p.s(lambda e, ro0=ro0: e.activation(ro0, ps_rs[0:1, 0:SG], AF.Abs), inc="act")
                    p.s(lambda e, ro3=ro3, hd=hd, i=i: e.activation(
                        ro3, mslice[:], AF.Exp, scale=-1.0,
                        bias=atab[0:1, hd * 8 + i:hd * 8 + i + 1]), inc="act")
                    p.wait("vector", "act")
                    p.wait("vector", "ld", ld_sg)
                    p.v(lambda e, ro0=ro0, ro3=ro3: e.tensor_mul(ro0, ro0, ro3), inc="dve")
                    p.wait("vector", "dve")
                    p.v(lambda e, ro0=ro0, ro1=ro1: e.tensor_max(ro1, ro0, eslice[:]), inc="dve")
                    p.wait("vector", "dve")
                    p.v(lambda e, ro1=ro1: e.tensor_scalar_add(ro1, ro1, 1e-6), inc="dve")
                    p.wait("vector", "dve")
                    p.v(lambda e, ro1=ro1, ro2=ro2: e.reciprocal(ro2, ro1), inc="dve")
                    p.wait("vector", "dve")
                    p.v(lambda e, ro2=ro2, ro3=ro3: e.tensor_mul(ro2, ro2, ro3), inc="dve")
                    p.wait("tensor", "dve")
                    p.guard_wait("tensor", "psbc")
                    p.t(lambda e, ro2=ro2: e.matmul(ps_bc[:, 0:SG], onesrow[:], ro2, start=True, stop=True), inc="mm")
                    p.wait("vector", "mm")
                    p.v(lambda e: e.tensor_copy(nfb[:], ps_bc[:, 0:SG]), inc="dve")
                    p.guard_set("psbc", "dve")
                    nfb_ready = cnt["dve"]
                    p.wait("vector", "dve", nfb_ready)
                    p.guard_wait("vector", "htrbuf")
                    for dc in range(4):
                        p.v(lambda e, dc=dc: e.tensor_mul(
                            htr[:, dc * SG:(dc + 1) * SG], HPS[dc][:, 0:SG], nfb[:]),
                            inc=("dve" if dc == 3 else None))
                    htr_ready = cnt["dve"]
                    p.guard_set("hps", "dve")
                    p.wait("tensor", "dve", htr_ready)
                    for dc in range(4):
                        p.t(lambda e, dc=dc: e.matmul(
                            ps_rs[0:1, 0:SG], onescol_b[:, 0:1], htr[:, dc * SG:(dc + 1) * SG],
                            start=(dc == 0), stop=(dc == 3)), inc=("mm" if dc == 3 else None))
                    p.wait("scalar", "dve", htr_ready)
                    for dc in range(4):
                        sq_ = sqh if dc % 2 == 0 else sqh2
                        if dc >= 2:
                            p.wait("scalar", "mm", cnt["mm"])
                        p.s(lambda e, dc=dc, sq_=sq_: e.activation(
                            sq_[:], htr[:, dc * SG:(dc + 1) * SG], AF.Square), inc="act")
                        p.wait("tensor", "act")
                        p.t(lambda e, dc=dc, sq_=sq_: e.matmul(
                            ps_rs[32:33, 0:SG], onescol[:, 0:1], sq_[:],
                            start=(dc == 0), stop=(dc == 3)), inc="mm")
                    p.wait("vector", "mm")
                    p.v(lambda e, ro0=ro0: e.tensor_scalar_mul(ro0, ps_rs[0:1, 0:SG], 1.0 / DH))
                    p.v(lambda e, ro1=ro1: e.tensor_scalar_mul(ro1, ps_rs[32:33, 0:SG], 1.0 / DH), inc="dve")
                    p.guard_set("psrs", "dve")
                    p.wait("vector", "dve")
                    p.v(lambda e, ro0=ro0, ro2=ro2: e.tensor_mul(ro2, ro0, ro0), inc="dve")
                    p.wait("vector", "dve")
                    p.v(lambda e, ro1=ro1, ro2=ro2: e.tensor_sub(ro1, ro1, ro2), inc="dve")
                    p.wait("vector", "dve")
                    p.v(lambda e, ro1=ro1: e.tensor_scalar_add(ro1, ro1, 1e-6), inc="dve")
                    p.wait("scalar", "dve")
                    p.s(lambda e, ro1=ro1, ro2=ro2: e.activation(ro2, ro1, AF.Sqrt), inc="act")
                    p.wait("vector", "act")
                    p.v(lambda e, ro1=ro1, ro2=ro2: e.reciprocal(ro1, ro2), inc="dve")
                    p.wait("tensor", "dve")
                    p.guard_wait("tensor", "psbc")
                    p.t(lambda e, ro0=ro0: e.matmul(ps_bc[:, 0:SG], onesrow[:], ro0, start=True, stop=True), inc="mm")
                    p.wait("vector", "mm")
                    p.v(lambda e: e.tensor_copy(mub[:], ps_bc[:, 0:SG]), inc="dve")
                    p.wait("tensor", "dve")
                    p.t(lambda e, ro1=ro1: e.matmul(ps_bc[:, 0:SG], onesrow[:], ro1, start=True, stop=True), inc="mm")
                    p.wait("vector", "mm")
                    p.v(lambda e: e.tensor_copy(r1b[:], ps_bc[:, 0:SG]), inc="dve")
                    p.guard_set("psbc", "dve")
                    p.wait("vector", "dve")
                    p.wait("vector", "ld", ld_sg)
                    p.guard_wait("vector", "hstg")
                    for dc in range(4):
                        cix = hd * 4 + dc
                        p.v(lambda e, dc=dc: e.tensor_sub(
                            htr[:, dc * SG:(dc + 1) * SG], htr[:, dc * SG:(dc + 1) * SG], mub[:]))
                        p.v(lambda e, dc=dc, cix=cix: e.scalar_tensor_tensor(
                            htr[:, dc * SG:(dc + 1) * SG], htr[:, dc * SG:(dc + 1) * SG],
                            normw[:, cix:cix + 1], r1b[:], ALU.mult, ALU.mult))
                        p.v(lambda e, dc=dc, cix=cix: e.scalar_tensor_tensor(
                            htr[:, dc * SG:(dc + 1) * SG], xabuf[:, dc * SG:(dc + 1) * SG],
                            skip_t[:, cix:cix + 1], htr[:, dc * SG:(dc + 1) * SG], ALU.mult, ALU.add))
                        p.v(lambda e, dc=dc: e.tensor_mul(
                            hstg[:, dc * SG:(dc + 1) * SG], htr[:, dc * SG:(dc + 1) * SG],
                            szbuf[:, dc * SG:(dc + 1) * SG]),
                            inc=("dve" if dc == 3 else None))
                    p.guard_set("htrbuf", "dve")
                    p.guard_set("hps", "dve")
                    p.wait("gpsimd", "dve")
                    for dc in range(4):
                        p.g(lambda e, hd=hd, dc=dc, s0=s0: e.dma_start(
                            out=SC["hst_s"][hd * 512 + dc * 128:hd * 512 + (dc + 1) * 128, s0:s0 + SG],
                            in_=hstg[:, dc * SG:(dc + 1) * SG]), inc="st")
                    p.guard_set("hstg", "gst")
                    p.guard_set("eibuf", "mm")
                    p.guard_set("xaszbuf", "dve")
                    p.guard_set("gsl", "act")
                p.guard_set("p2bufs", "mm")
            if debug:
                p.wait("gpsimd", "gst")
                p.g(lambda e: e.dma_start(out=dbg["d_hst"], in_=SC["hst_s"]), inc="st")
                p.wait("gpsimd", "gst")
            p.flush(blk)
        P2.close()

    if 2 in phases:
        P2b = ExitStack()
        wdnb = sb(P2b, "wdnb", (128, 16 * E), BF16)
        hstb = sb(P2b, "hstb", (128, 16 * TP1), BF16)
        xtf = sb(P2b, "xtf", (128, 8 * TP1))
        x2st = sb(P2b, "x2st", (128, 4 * TP1))
        ps_w0 = psum(P2b, "ps_w0", (128, 512))
        ps_w1 = psum(P2b, "ps_w1", (128, 512))
        with nc.Block() as blk:
            p.wait("sync", "gst", cnt["gst"])
            p.g(lambda e: e.dma_start(
                out=wdnb[:].rearrange("p (c n) -> p c n", c=16),
                in_=D["wdn"].rearrange("(c p) n -> p c n", p=128)), inc="gld")
            p.wait("tensor", "gld")
            for t in range(NT1):
                t0 = t * TP1
                p.guard_wait("sync", "p2bldh")
                p.sy(lambda e, t0=t0: e.dma_start(
                    out=hstb[:].rearrange("p (c n) -> p c n", c=16),
                    in_=SC["hst_s"].rearrange("(c p) n -> p c n", p=128)[:, :, t0:t0 + TP1]))
                p.sy(lambda e, t0=t0: e.dma_start(
                    out=xtf[:].rearrange("p (c n) -> p c n", c=8),
                    in_=D["xT"].rearrange("(c p) n -> p c n", p=128)[:, :, t0:t0 + TP1]))
                ld_t = cnt["ld"]
                p.wait("tensor", "ld", ld_t)
                for dc in range(8):
                    pb = ps_w0 if dc % 2 == 0 else ps_w1
                    p.guard_wait("tensor", f"wdb{dc % 2}")
                    for hc in range(16):
                        p.t(lambda e, dc=dc, hc=hc, pb=pb: e.matmul(
                            pb[:, 0:TP1], wdnb[:, hc * E + dc * 128:hc * E + (dc + 1) * 128],
                            hstb[:, hc * TP1:(hc + 1) * TP1],
                            start=(hc == 0), stop=(hc == 15)), inc=("mm" if hc == 15 else None))
                    mmv = cnt["mm"]
                    sl = dc % 4
                    p.wait("vector", "mm", mmv)
                    p.guard_wait("vector", f"x2st{sl}")
                    p.v(lambda e, dc=dc, pb=pb, sl=sl: e.scalar_tensor_tensor(
                        x2st[:, sl * TP1:(sl + 1) * TP1], pb[:, 0:TP1], modv[:, 16 + dc:17 + dc],
                        xtf[:, dc * TP1:(dc + 1) * TP1], ALU.mult, ALU.add), inc="dve")
                    p.guard_set(f"wdb{dc % 2}", "dve")
                    p.wait("gpsimd", "dve")
                    p.g(lambda e, dc=dc, t0=t0, sl=sl: e.dma_start(
                        out=SC["x2_s"][dc * 128:(dc + 1) * 128, t0:t0 + TP1],
                        in_=x2st[:, sl * TP1:(sl + 1) * TP1]), inc="st")
                    p.guard_set(f"x2st{sl}", "gst")
                p.guard_set("p2bldh", "mm")
            if debug:
                p.wait("gpsimd", "gst")
                p.g(lambda e: e.dma_start(out=dbg["d_x2"], in_=SC["x2_s"]), inc="st")
                p.wait("gpsimd", "gst")
            p.flush(blk)
        P2b.close()

    if 3 in phases:
        P3 = ExitStack()
        TW = TP3 + 2
        w1qb = sb(P3, "w1qb", (128, 8 * DHID), BF16)
        w2qb = sb(P3, "w2qb", (128, 32 * E), BF16)
        x2t = sb(P3, "x2t", (128, 8 * TW), BF16)
        x2res = sb(P3, "x2res", (128, 2 * TP3))
        sqa = sb(P3, "sqa", (128, TW))
        sqb = sb(P3, "sqb", (128, TW))
        rows3 = sb(P3, "rows3", (1, 3 * TW))
        h2t = sb(P3, "h2t", (128, 8 * TW), BF16)
        runm = sb(P3, "runm", (128, TW), BF16)
        runn = sb(P3, "runn", (128, TW), BF16)
        colsA = sb(P3, "colsA", (128, 72))
        colsB = sb(P3, "colsB", (128, 72))
        colsC = sb(P3, "colsC", (128, 72))
        colsD = sb(P3, "colsD", (128, 72))
        colsE = sb(P3, "colsE", (128, 72))
        scb = sb(P3, "scb", (128, TW), BF16)
        rfb = sb(P3, "rfb", (128, TW))
        xq1 = sb(P3, "xq1", (128, 8 * TW), BF16)
        h1t = sb(P3, "h1t", (128, 32 * TW), BF16)
        cacc3 = sb(P3, "cacc3", (128, TW))
        xst = sb(P3, "xst", (128, 32 * TP3), BF16)
        xq2 = sb(P3, "xq2", (128, 32 * TP3), BF16)
        outst = sb(P3, "outst", (128, TP3))
        yq1 = sqa
        ym1 = sb(P3, "ym1", (128, TW))
        ym2 = sqb
        ps_m0 = psum(P3, "ps_m0", (128, 512))
        ps_m1 = psum(P3, "ps_m1", (128, 512))
        ps_r3 = psum(P3, "ps_r3", (128, 512))
        ps_b3 = psum(P3, "ps_b3", (128, 512))
        ps_t3 = psum(P3, "ps_t3", (128, 512))

        def quant_cols(p, srccols, n, wi):
            pass

        with nc.Block() as blk:
            p.wait("sync", "gst", cnt["gst"])
            p.g(lambda e: e.dma_start(
                out=w1qb[:].rearrange("p (c n) -> p c n", c=8),
                in_=SC["w1q_s"].rearrange("(c p) n -> p c n", p=128)), inc="gld")
            p.g(lambda e: e.dma_start(
                out=w2qb[:].rearrange("p (c n) -> p c n", c=32),
                in_=SC["w2q_s"].rearrange("(c p) n -> p c n", p=128)), inc="gld")
            p.wait("tensor", "gld")
            for buf, wd in ((h2t, 8 * TW), (xq1, 8 * TW), (runm, TW), (runn, TW),
                            (colsA, 72), (colsB, 72), (colsC, 72), (colsD, 72), (colsE, 72)):
                p.v(lambda e, buf=buf, wd=wd: e.memset(buf[:, 0:wd], 0.0))
            p.v(lambda e: e.memset(rows3[:], 0.0), inc="dve")
            p.wait("vector", "dve")
            p.wait("tensor", "dve")
            p.wait("scalar", "dve")
            for t in range(NT3):
                t0 = t * TP3
                lo = 1 if t == 0 else 0
                hi = 1 if t == NT3 - 1 else 0
                a0 = t0 - 1 + lo
                a1 = t0 + TP3 + 1 - hi
                nw_ = a1 - a0
                off = lo
                p.guard_wait("gpsimd", "x2t")
                p.g(lambda e, a0=a0, nw_=nw_, off=off: e.dma_start(
                    out=x2t[:].rearrange("p (c n) -> p c n", c=8)[:, :, off:off + nw_],
                    in_=SC["x2_s"].rearrange("(c p) n -> p c n", p=128)[:, :, a0:a0 + nw_]), inc="gld")
                ld_t = cnt["gld"]
                p.wait("tensor", "gld", ld_t)
                p.guard_wait("tensor", "psr3")
                for c in range(8):
                    p.t(lambda e, c=c, off=off, nw_=nw_: e.matmul(
                        ps_r3[0:1, 0:nw_], onescol_b[:, 0:1], x2t[:, c * TW + off:c * TW + off + nw_],
                        start=(c == 0), stop=(c == 7)), inc=("mm" if c == 7 else None))
                p.wait("scalar", "gld", ld_t)
                for c in range(8):
                    d_ = sqa if c % 2 == 0 else sqb
                    if c >= 2:
                        p.wait("scalar", "mm", cnt["mm"])
                    p.s(lambda e, c=c, d=d_, off=off, nw_=nw_: e.activation(
                        d[:, off:off + nw_], x2t[:, c * TW + off:c * TW + off + nw_], AF.Square), inc="act")
                    p.wait("tensor", "act")
                    p.t(lambda e, c=c, d=d_, off=off, nw_=nw_: e.matmul(
                        ps_r3[32:33, 0:nw_], onescol[:, 0:1], d[:, off:off + nw_],
                        start=(c == 0), stop=(c == 7)), inc="mm")
                p.wait("vector", "mm")
                u0 = rows3[0:1, 0:TW]; u1 = rows3[0:1, TW:2 * TW]; u2 = rows3[0:1, 2 * TW:3 * TW]
                p.v(lambda e, u0=u0, nw_=nw_: e.tensor_scalar_mul(u0[0:1, 0:nw_], ps_r3[0:1, 0:nw_], 1.0 / E))
                p.v(lambda e, u1=u1, nw_=nw_: e.tensor_scalar_mul(u1[0:1, 0:nw_], ps_r3[32:33, 0:nw_], 1.0 / E), inc="dve")
                p.guard_set("psr3", "dve")
                p.wait("vector", "dve")
                p.v(lambda e, u0=u0, u2=u2, nw_=nw_: e.tensor_mul(u2[0:1, 0:nw_], u0[0:1, 0:nw_], u0[0:1, 0:nw_]), inc="dve")
                p.wait("vector", "dve")
                p.v(lambda e, u1=u1, u2=u2, nw_=nw_: e.tensor_sub(u1[0:1, 0:nw_], u1[0:1, 0:nw_], u2[0:1, 0:nw_]), inc="dve")
                p.wait("vector", "dve")
                p.v(lambda e, u1=u1, nw_=nw_: e.tensor_scalar_add(u1[0:1, 0:nw_], u1[0:1, 0:nw_], 1e-5), inc="dve")
                p.wait("scalar", "dve")
                p.s(lambda e, u1=u1, u2=u2, nw_=nw_: e.activation(u2[0:1, 0:nw_], u1[0:1, 0:nw_], AF.Sqrt), inc="act")
                p.wait("vector", "act")
                p.v(lambda e, u1=u1, u2=u2, nw_=nw_: e.reciprocal(u1[0:1, 0:nw_], u2[0:1, 0:nw_]), inc="dve")
                p.wait("vector", "dve")
                p.v(lambda e, u0=u0, u1=u1, u2=u2, nw_=nw_: e.tensor_mul(u2[0:1, 0:nw_], u0[0:1, 0:nw_], u1[0:1, 0:nw_]), inc="dve")
                p.wait("vector", "dve")
                p.v(lambda e, u2=u2, nw_=nw_: e.tensor_scalar_mul(u2[0:1, 0:nw_], u2[0:1, 0:nw_], -1.0), inc="dve")
                p.wait("tensor", "dve")
                p.guard_wait("tensor", "psb3")
                p.t(lambda e, u1=u1, nw_=nw_: e.matmul(ps_b3[:, 0:nw_], onesrow[:], u1[0:1, 0:nw_], start=True, stop=True))
                p.t(lambda e, u2=u2, nw_=nw_: e.matmul(ps_t3[:, 0:nw_], onesrow[:], u2[0:1, 0:nw_], start=True, stop=True), inc="mm")
                p.wait("vector", "mm")
                p.guard_wait("vector", "h2t")
                for c in range(8):
                    p.v(lambda e, c=c, off=off, nw_=nw_: e.tensor_mul(
                        sqa[:, off:off + nw_], x2t[:, c * TW + off:c * TW + off + nw_], ps_b3[:, 0:nw_]))
                    p.v(lambda e, nw_=nw_, off=off: e.tensor_add(
                        sqa[:, off:off + nw_], sqa[:, off:off + nw_], ps_t3[:, 0:nw_]))
                    p.v(lambda e, c=c, off=off, nw_=nw_: e.tensor_scalar(
                        h2t[:, c * TW + off:c * TW + off + nw_], sqa[:, off:off + nw_],
                        modv[:, 32 + c:33 + c], modv[:, 24 + c:25 + c], ALU.mult, ALU.add),
                        inc=("dve" if c == 7 else None))
                p.guard_set("psb3", "dve")
                h2_done = cnt["dve"]
                p.wait("vector", "dve", h2_done)
                p.guard_wait("vector", "runm")
                p.v(lambda e, off=off, nw_=nw_: e.tensor_copy(runm[:, off:off + nw_], h2t[:, off:off + nw_]))
                p.v(lambda e, off=off, nw_=nw_: e.tensor_copy(runn[:, off:off + nw_], h2t[:, off:off + nw_]), inc="dve")
                for c in range(1, 8):
                    p.wait("vector", "dve")
                    p.v(lambda e, c=c, off=off, nw_=nw_: e.tensor_tensor(
                        runm[:, off:off + nw_], runm[:, off:off + nw_],
                        h2t[:, c * TW + off:c * TW + off + nw_], ALU.max))
                    p.v(lambda e, c=c, off=off, nw_=nw_: e.tensor_tensor(
                        runn[:, off:off + nw_], runn[:, off:off + nw_],
                        h2t[:, c * TW + off:c * TW + off + nw_], ALU.min), inc="dve")
                p.wait("vector", "dve")
                p.v(lambda e, off=off, nw_=nw_: e.tensor_scalar_mul(runn[:, off:off + nw_], runn[:, off:off + nw_], -1.0), inc="dve")
                p.wait("vector", "dve")
                p.v(lambda e, off=off, nw_=nw_: e.tensor_max(runm[:, off:off + nw_], runm[:, off:off + nw_], runn[:, off:off + nw_]), inc="dve")
                p.wait("tensor", "dve")

                def colquant(wi, src_bf, dst_s29_col, dst_rf_col, nchunks):
                    p.guard_wait("tensor", "pst3q")
                    nb = (TW + 127) // 128
                    for b_ in range(nb):
                        w_ = min(128, TW - b_ * 128)
                        p.t(lambda e, b_=b_, w_=w_, src_bf=src_bf: e.matmul(
                            ps_t3[0:w_, b_ * 128:b_ * 128 + 128],
                            src_bf[:, b_ * 128:b_ * 128 + w_], identb[:],
                            start=True, stop=True), inc=("mm" if b_ == nb - 1 else None))
                    p.wait("vector", "mm")
                    for b_ in range(nb):
                        w_ = min(128, TW - b_ * 128)
                        p.v(lambda e, b_=b_, w_=w_: e.reduce_max(
                            colsA[0:w_, 32 * b_:32 * b_ + 1],
                            ps_t3[0:w_, b_ * 128:b_ * 128 + 128], axis=AX.X), inc=("dve" if b_ == nb - 1 else None))
                    p.guard_set("pst3q", "dve")
                    p.wait("vector", "dve")
                    p.v(lambda e: e.tensor_scalar_max(colsB[:, 0:72], colsA[:, 0:72], 1e-5), inc="dve")
                    p.wait("vector", "dve")
                    p.v(lambda e: e.reciprocal(colsC[:, 0:72], colsB[:, 0:72]), inc="dve")
                    p.wait("vector", "dve")
                    p.v(lambda e: e.tensor_scalar_mul(colsD[:, 0:72], colsC[:, 0:72], 127.0 * S29))
                    p.v(lambda e, wi=wi: e.tensor_scalar(
                        colsE[:, 0:72], colsB[:, 0:72], mwcol[:, wi:wi + 1], P29 / 127.0,
                        ALU.mult, ALU.mult), inc="dve")
                    p.wait("tensor", "dve")
                    for which, ct in ((0, colsD), (1, colsE)):
                        p.t(lambda e, ct=ct: e.matmul(
                            ps_t3[0:65, 0:128], ct[:, 0:65], ident[:, 0:128], start=True, stop=True), inc="mm")
                        p.wait("vector", "mm")
                        for b_ in range(nb):
                            w_ = min(128, TW - b_ * 128)
                            ro_off = 0 if which == 0 else TW
                            p.v(lambda e, b_=b_, w_=w_, ro_off=ro_off: e.tensor_copy(
                                rows3[0:1, ro_off + b_ * 128:ro_off + b_ * 128 + w_],
                                ps_t3[32 * b_:32 * b_ + 1, 0:w_]), inc=("dve" if b_ == nb - 1 else None))
                        p.wait("tensor", "dve")
                    p.t(lambda e: e.matmul(ps_b3[:, 0:TW], onesrow[:], rows3[0:1, 0:TW], start=True, stop=True), inc="mm")
                    p.wait("vector", "mm")
                    p.guard_wait("vector", "scbbuf")
                    p.v(lambda e: e.tensor_copy(scb[:], ps_b3[:, 0:TW]), inc="dve")
                    p.wait("tensor", "dve")
                    p.t(lambda e: e.matmul(ps_b3[:, 0:TW], onesrow[:], rows3[0:1, TW:2 * TW], start=True, stop=True), inc="mm")
                    p.wait("vector", "mm")
                    p.guard_wait("vector", "rfbbuf")
                    p.v(lambda e: e.tensor_copy(rfb[:], ps_b3[:, 0:TW]), inc="dve")

                colquant(0, runm, 0, 8, 8)
                p.wait("vector", "dve")
                for c in range(8):
                    p.v(lambda e, c=c: e.tensor_mul(yq1[:], h2t[:, c * TW:(c + 1) * TW], scb[:]))
                    p.v(lambda e: e.tensor_scalar(ym1[:], yq1[:], CP, CP, ALU.add, ALU.subtract))
                    p.v(lambda e: e.tensor_scalar(ym2[:], yq1[:], C23, C23, ALU.add, ALU.subtract))
                    p.v(lambda e, c=c: e.scalar_tensor_tensor(
                        xq1[:, c * TW:(c + 1) * TW], ym1[:], omi[:, 0:1], ym2[:], ALU.mult, ALU.add),
                        inc=("dve" if c == 7 else None))
                xq_done = cnt["dve"]
                p.guard_set("scbbuf", "dve")
                p.wait("tensor", "dve", xq_done)
                for hc in range(32):
                    pb = ps_m0 if hc % 2 == 0 else ps_m1
                    p.guard_wait("tensor", f"m3b{hc % 2}")
                    for kc in range(8):
                        p.t(lambda e, hc=hc, kc=kc, pb=pb: e.matmul(
                            pb[:, 0:TW], w1qb[:, kc * DHID + hc * 128:kc * DHID + (hc + 1) * 128],
                            xq1[:, kc * TW:(kc + 1) * TW],
                            start=(kc == 0), stop=(kc == 7)), inc=("mm" if kc == 7 else None))
                    mmv = cnt["mm"]
                    p.wait("vector", "mm", mmv)
                    if hc == 0:
                        p.guard_wait("vector", "h1t")
                    p.v(lambda e, hc=hc, pb=pb: e.tensor_mul(
                        h1t[:, hc * TW:(hc + 1) * TW], pb[:, 0:TW], rfb[:]), inc="dve")
                    p.guard_set(f"m3b{hc % 2}", "dve")
                h1_done = cnt["dve"]
                p.guard_set("rfbbuf", "dve")
                if t == 0 or t == NT3 - 1:
                    p.wait("vector", "dve", h1_done)
                    for c in range(32):
                        col = c * TW + (0 if t == 0 else TW - 1)
                        p.v(lambda e, col=col: e.tensor_scalar_mul(
                            h1t[:, col:col + 1], onescol[:, 0:1], 0.0), inc=("dve" if c == 31 else None))
                p.wait("vector", "dve")
                p.guard_wait("vector", "xstbuf")
                for c in range(32):
                    b_ = c * TW
                    cc_ = cacc3 if c % 2 == 0 else ym1
                    p.guard_wait("vector", f"cacc3{c % 2}")
                    p.v(lambda e, c=c, b_=b_, cc_=cc_: e.tensor_scalar(
                        cc_[:, 0:TP3], h1t[:, b_:b_ + TP3], conv2w[:, 0 * 32 + c:0 * 32 + c + 1], None, ALU.mult))
                    for tp_ in range(1, 3):
                        p.v(lambda e, c=c, b_=b_, tp_=tp_, cc_=cc_: e.scalar_tensor_tensor(
                            cc_[:, 0:TP3], h1t[:, b_ + tp_:b_ + tp_ + TP3], conv2w[:, tp_ * 32 + c:tp_ * 32 + c + 1],
                            cc_[:, 0:TP3], ALU.mult, ALU.add), inc=("dve" if tp_ == 2 else None))
                    p.wait("scalar", "dve")
                    p.s(lambda e, c=c, cc_=cc_: e.activation(
                        xst[:, c * TP3:(c + 1) * TP3], cc_[:, 0:TP3], AF.Silu,
                        bias=conv2b[:, c:c + 1], scale=1.0), inc="act")
                    p.guard_set(f"cacc3{c % 2}", "act")
                    p.wait("vector", "act")
                    if c == 0:
                        p.v(lambda e: e.tensor_copy(runm[:, 0:TP3], xst[:, 0:TP3]))
                        p.v(lambda e: e.tensor_copy(runn[:, 0:TP3], xst[:, 0:TP3]), inc="dve")
                    else:
                        p.v(lambda e, c=c: e.tensor_tensor(
                            runm[:, 0:TP3], runm[:, 0:TP3], xst[:, c * TP3:(c + 1) * TP3], ALU.max))
                        p.v(lambda e, c=c: e.tensor_tensor(
                            runn[:, 0:TP3], runn[:, 0:TP3], xst[:, c * TP3:(c + 1) * TP3], ALU.min), inc="dve")
                p.wait("vector", "dve")
                p.v(lambda e: e.tensor_scalar_mul(runn[:, 0:TP3], runn[:, 0:TP3], -1.0), inc="dve")
                p.wait("vector", "dve")
                p.v(lambda e: e.tensor_max(runm[:, 0:TP3], runm[:, 0:TP3], runn[:, 0:TP3]), inc="dve")
                p.wait("tensor", "dve")
                colquant(1, runm, 0, 8, 32)
                p.wait("vector", "dve")
                for c in range(32):
                    p.v(lambda e, c=c: e.tensor_mul(yq1[:, 0:TP3], xst[:, c * TP3:(c + 1) * TP3], scb[:, 0:TP3]))
                    p.v(lambda e: e.tensor_scalar(ym1[:, 0:TP3], yq1[:, 0:TP3], CP, CP, ALU.add, ALU.subtract))
                    p.v(lambda e: e.tensor_scalar(ym2[:, 0:TP3], yq1[:, 0:TP3], C23, C23, ALU.add, ALU.subtract))
                    p.v(lambda e, c=c: e.scalar_tensor_tensor(
                        xq2[:, c * TP3:(c + 1) * TP3], ym1[:, 0:TP3], omi[:, 0:1], ym2[:, 0:TP3], ALU.mult, ALU.add),
                        inc=("dve" if c == 31 else None))
                xq2_done = cnt["dve"]
                p.guard_set("scbbuf", "dve")
                p.guard_set("xstbuf", "dve")
                p.wait("tensor", "dve", xq2_done)
                for dc in range(8):
                    p.guard_wait("sync", f"x2res{dc % 2}")
                    p.sy(lambda e, dc=dc, t0=t0: e.dma_start(
                        out=x2res[:, (dc % 2) * TP3:(dc % 2 + 1) * TP3],
                        in_=SC["x2_s"][dc * 128:(dc + 1) * 128, t0:t0 + TP3]))
                    x2r_ld = cnt["ld"]
                    pb = ps_m0 if dc % 2 == 0 else ps_m1
                    p.guard_wait("tensor", f"m3b{dc % 2}")
                    for kc in range(32):
                        p.t(lambda e, dc=dc, kc=kc, pb=pb: e.matmul(
                            pb[:, 0:TP3], w2qb[:, kc * E + dc * 128:kc * E + (dc + 1) * 128],
                            xq2[:, kc * TP3:(kc + 1) * TP3],
                            start=(kc == 0), stop=(kc == 31)), inc=("mm" if kc == 31 else None))
                    mmv = cnt["mm"]
                    sl = 0
                    p.wait("vector", "mm", mmv)
                    p.guard_wait("vector", f"outst{sl}")
                    p.v(lambda e, dc=dc, pb=pb: e.tensor_mul(ym1[:, 0:TP3], pb[:, 0:TP3], rfb[:, 0:TP3]), inc="dve")
                    p.guard_set(f"m3b{dc % 2}", "dve")
                    p.wait("vector", "dve")
                    p.wait("vector", "ld", x2r_ld)
                    p.v(lambda e, dc=dc, sl=sl: e.scalar_tensor_tensor(
                        outst[:, sl * TP3:(sl + 1) * TP3], ym1[:, 0:TP3], modv[:, 40 + dc:41 + dc],
                        x2res[:, (dc % 2) * TP3:(dc % 2 + 1) * TP3], ALU.mult, ALU.add), inc="dve")
                    p.guard_set(f"x2res{dc % 2}", "dve")
                    p.wait("gpsimd", "dve")
                    p.g(lambda e, dc=dc, t0=t0, sl=sl: e.dma_start(
                        out=yT[dc * 128:(dc + 1) * 128, t0:t0 + TP3],
                        in_=outst[:, sl * TP3:(sl + 1) * TP3]), inc="st")
                    p.guard_set(f"outst{sl}", "gst")
                p.guard_set("rfbbuf", "dve")
                p.guard_set("h1t", "mm")
                p.guard_set("h2t", "mm")
                p.guard_set("runm", "dve")
                p.guard_set("x2t", "dve")
            p.wait("gpsimd", "gst")
            p.wait("sync", "st", cnt["st"])
            p.flush(blk)
        P3.close()

    A.close()
    top.close()
    return nc, dbg, SC, D


_BUILD_CACHE = {}


def kernel(**inputs):
    maps = prep_inputs(inputs)
    in_maps = [{("i_" + k): v for k, v in m.items()} for m in maps]
    key = ("full", DEBUG)
    if key not in _BUILD_CACHE:
        _BUILD_CACHE[key] = build(debug=DEBUG)
    nc, dbg, SC, D = _BUILD_CACHE[key]
    res = run_bass_kernel_spmd(nc, in_maps, core_ids=list(range(B)))
    out = np.stack([np.ascontiguousarray(r["yT"].T) for r in res.results], axis=0)
    return out.astype(np.float32)


# version 10 (final)
# speedup vs baseline: 1.0810x; 1.0810x over previous
import math
from contextlib import ExitStack
import numpy as np
import ml_dtypes
import concourse.bass as bass
import concourse.mybir as mybir
from concourse.bass_utils import run_bass_kernel_spmd

F32 = mybir.dt.float32
BF16 = mybir.dt.bfloat16
AF = mybir.ActivationFunctionType
ALU = mybir.AluOpType
AX = mybir.AxisListType

DEBUG = False

B, S, E, INNER, NH, DH, DHID = 8, 4096, 1024, 2048, 4, 512, 4096
TP1, TP3, SG = 512, 256, 512
NT1, NT3, NSG, NKT = S // TP1, S // TP3, S // SG, S // 128

C23 = float(1.5 * 2.0 ** 23)
S29 = float(2.0 ** -29)
CP = float(1.5 * 2.0 ** -6)
P29 = float(2.0 ** 29)
LNDH = float(-0.5 * math.log(512.0))


class Prog:
    ENGS = ("sync", "tensor", "vector", "scalar", "gpsimd")

    def __init__(self, nc, sems, cnt):
        self.nc = nc
        self.streams = {e: [] for e in self.ENGS}
        self.sems = sems
        self.cnt = cnt
        self.guards = {}

    def _app(self, eng, fn, inc, inc_n):
        if inc is not None:
            h = self.sems[inc]
            self.streams[eng].append(lambda e, fn=fn, h=h, n=inc_n: fn(e).then_inc(h, n))
            self.cnt[inc] += inc_n
            return self.cnt[inc]
        self.streams[eng].append(fn)
        return None

    def sy(self, fn, inc="ld", inc_n=16):
        return self._app("sync", fn, inc, inc_n)

    def t(self, fn, inc=None, inc_n=1):
        return self._app("tensor", fn, inc, inc_n)

    def v(self, fn, inc=None, inc_n=1):
        return self._app("vector", fn, inc, inc_n)

    def s(self, fn, inc=None, inc_n=1):
        return self._app("scalar", fn, inc, inc_n)

    def g(self, fn, inc="st", inc_n=16):
        if inc == "st":
            inc = "gst"
        return self._app("gpsimd", fn, inc, inc_n)

    def wait(self, eng, sem, value=None):
        v = self.cnt[sem] if value is None else value
        h = self.sems[sem]
        self.streams[eng].append(lambda e, h=h, v=v: e.wait_ge(h, v))

    def guard_set(self, key, sem):
        self.guards[key] = (sem, self.cnt[sem])

    def guard_wait(self, eng, key):
        if key in self.guards:
            sem, v = self.guards.pop(key)
            self.wait(eng, sem, v)

    def flush(self, block):
        for name in self.ENGS:
            thunks = self.streams[name]
            if not thunks:
                continue

            def runner(e, thunks=tuple(thunks)):
                for fn in thunks:
                    fn(e)

            getattr(block, name)(runner)
            self.streams[name] = []


def _blockdiag(w, transposed):
    out = np.zeros((16, 128, 128), np.float32)
    for h in range(512):
        c, hl = divmod(h, 32)
        blk = w[h] if transposed else np.transpose(w[h], (1, 0))
        out[c, hl * 4:hl * 4 + 4, hl * 4:hl * 4 + 4] = blk
    return out.reshape(2048, 128)


def prep_inputs(inp):
    f32 = np.float32
    x = np.asarray(inp["L0_attn_x"], f32)
    sh = {}
    sh["wup"] = np.ascontiguousarray(np.asarray(inp["L0_attn_W_up"], f32))
    sh["wdn"] = np.ascontiguousarray(np.asarray(inp["L0_attn_W_down"], f32))
    sh["w1T"] = np.ascontiguousarray(np.asarray(inp["L0_ffn_w1"], f32).T)
    sh["w2T"] = np.ascontiguousarray(np.asarray(inp["L0_ffn_w2"], f32).T)
    sh["wmod"] = np.ascontiguousarray(np.asarray(inp["L0_wmod"], f32))
    sh["wqbd"] = _blockdiag(np.asarray(inp["L0_attn_Wq"], f32), False)
    sh["wkbd"] = _blockdiag(np.asarray(inp["L0_attn_Wk"], f32), False)
    sh["wvbd"] = _blockdiag(np.asarray(inp["L0_attn_Wv"], f32), False)
    sh["wqbdT"] = _blockdiag(np.asarray(inp["L0_attn_Wq"], f32), True)
    sh["wkbdT"] = _blockdiag(np.asarray(inp["L0_attn_Wk"], f32), True)
    sh["wvbdT"] = _blockdiag(np.asarray(inp["L0_attn_Wv"], f32), True)
    sh["wig"] = np.ascontiguousarray(
        np.concatenate([np.asarray(inp["L0_attn_W_ig"], f32),
                        np.asarray(inp["L0_attn_W_fg"], f32)], axis=1))
    sh["bigfg"] = np.ascontiguousarray(np.concatenate(
        [np.asarray(inp["L0_attn_b_ig"], f32),
         np.asarray(inp["L0_attn_b_fg"], f32)]).reshape(8, 1))
    cw = np.asarray(inp["L0_attn_conv_w"], f32)
    c1 = np.zeros((128, 64), f32)
    for tap in range(4):
        for c in range(16):
            c1[:, tap * 16 + c] = cw[tap, c * 128:(c + 1) * 128]
    sh["conv1w"] = c1
    sh["conv1b"] = np.ascontiguousarray(np.asarray(inp["L0_attn_conv_b"], f32).reshape(16, 128).T)
    sh["normw"] = np.ascontiguousarray(np.asarray(inp["L0_attn_norm_w"], f32).reshape(16, 128).T)
    sh["skip"] = np.ascontiguousarray(np.asarray(inp["L0_attn_skip"], f32).reshape(16, 128).T)
    c2 = np.asarray(inp["L0_ffn_conv_w"], f32).reshape(4096, 3)
    cc2 = np.zeros((128, 96), f32)
    for tap in range(3):
        for c in range(32):
            cc2[:, tap * 32 + c] = c2[c * 128:(c + 1) * 128, tap]
    sh["conv2w"] = cc2
    sh["conv2b"] = np.ascontiguousarray(np.asarray(inp["L0_ffn_conv_b"], f32).reshape(32, 128).T)
    sh["onesrow"] = np.ones((1, 128), f32)
    sh["negonesrow"] = -np.ones((1, 128), f32)
    sh["onescol"] = np.ones((128, 4), f32)
    sh["ident"] = np.eye(128, dtype=f32)
    masks = np.zeros((128, 4 * SG), ml_dtypes.bfloat16)
    for r in range(4):
        ti = 128 * r + np.arange(128)[:, None]
        si = np.arange(SG)[None, :]
        masks[:, r * SG:(r + 1) * SG] = (ti <= si).astype(ml_dtypes.bfloat16)
    sh["masks"] = masks
    cmat = np.asarray(inp["L0_c"], f32)
    maps = []
    for b in range(B):
        m = dict(sh)
        m["xT"] = np.ascontiguousarray(x[b].T)
        m["crow"] = np.ascontiguousarray(cmat[b].reshape(8, 128).T)
        maps.append(m)
    return maps


def build(debug=False, phases=(0, 1, 2, 3)):
    nc = bass.Bass("TRN2", target_bir_lowering=False, detect_race_conditions=False)
    top = ExitStack()

    def dram(name, shape, dt=F32, kind="ExternalInput"):
        return nc.dram_tensor(name, list(shape), dt, kind=kind).ap()

    def sb(st, name, shape, dt=F32):
        return st.enter_context(nc.sbuf_tensor(name, list(shape), dt))

    def psum(st, name, shape, dt=F32):
        return st.enter_context(nc.psum_tensor(name, list(shape), dt))

    D = {}
    for nm, shp, dt in [
        ("xT", (E, S), F32), ("wup", (E, DHID), F32), ("wdn", (INNER, E), F32),
        ("w1T", (E, DHID), F32), ("w2T", (DHID, E), F32), ("wmod", (E, 6 * E), F32),
        ("wqbd", (INNER, 128), F32), ("wkbd", (INNER, 128), F32), ("wvbd", (INNER, 128), F32),
        ("wqbdT", (INNER, 128), F32), ("wkbdT", (INNER, 128), F32), ("wvbdT", (INNER, 128), F32),
        ("wig", (3 * INNER, 8), F32), ("bigfg", (8, 1), F32),
        ("conv1w", (128, 64), F32), ("conv1b", (128, 16), F32),
        ("normw", (128, 16), F32), ("skip", (128, 16), F32),
        ("conv2w", (128, 96), F32), ("conv2b", (128, 32), F32),
        ("onesrow", (1, 128), F32), ("negonesrow", (1, 128), F32),
        ("onescol", (128, 4), F32), ("ident", (128, 128), F32),
        ("masks", (128, 4 * SG), BF16), ("crow", (128, 8), F32),
    ]:
        D[nm] = dram("i_" + nm, shp, dt)
    SC = {}
    for nm, shp, dt in [
        ("modscr", (1, 6144), F32),
        ("q_s", (INNER, S), BF16), ("k_s", (INNER, S), BF16),
        ("ktok_s", (NH * S, DH), BF16), ("vtok_s", (NH * S, DH), BF16),
        ("xact_s", (INNER, S), BF16), ("sz_s", (INNER, S), BF16),
        ("hst_s", (INNER, S), BF16), ("x2_s", (E, S), F32),
        ("w1q_s", (E, DHID), BF16), ("w2q_s", (DHID, E), BF16),
        ("gates_s", (8, S), F32),
    ]:
        SC[nm] = dram("s_" + nm, shp, dt, kind="Internal")
    yT = dram("yT", (E, S), F32, kind="ExternalOutput")
    dbg = {}
    if debug:
        for nm, shp, dt in [("d_xact", (INNER, S), F32), ("d_q", (INNER, S), F32),
                            ("d_k", (INNER, S), F32), ("d_sz", (INNER, S), F32),
                            ("d_igfg", (8, S), F32), ("d_vtok", (NH * S, DH), F32),
                            ("d_mod", (128, 48), F32), ("d_G", (NH, S), F32),
                            ("d_M", (NH, S), F32), ("d_espm", (NH, S), F32),
                            ("d_w1q", (E, DHID), F32), ("d_w2q", (DHID, E), F32),
                            ("d_sw", (1, 4), F32), ("d_hst", (INNER, S), F32),
                            ("d_x2", (E, S), F32), ("d_omi", (128, 4), F32)]:
            dbg[nm] = dram(nm, shp, dt, kind="ExternalOutput")

    sems, cnt = {}, {}
    for n in ["ld", "gld", "mm", "dve", "act", "gp", "st", "gst", "ev", "aux", "fin"]:
        sems[n] = top.enter_context(nc.semaphore(n))
        cnt[n] = 0

    R = top
    onesrow = sb(R, "onesrow", (1, 128))
    negonesrow = sb(R, "negonesrow", (1, 128))
    onescol = sb(R, "onescol", (128, 4))
    thrcol = sb(R, "thrcol", (128, 2))
    onescol_b = sb(R, "onescol_b", (128, 2), BF16)
    ident = sb(R, "ident", (128, 128))
    conv1w = sb(R, "conv1w", (128, 64))
    conv1b = sb(R, "conv1b", (128, 16))
    normw = sb(R, "normw", (128, 16))
    skip_t = sb(R, "skip_t", (128, 16))
    conv2w = sb(R, "conv2w", (128, 96))
    conv2b = sb(R, "conv2b", (128, 32))
    modv = sb(R, "modv", (128, 48))
    omi = sb(R, "omi", (128, 4))
    swrow = sb(R, "swrow", (1, 4))
    weffa = sb(R, "weffa", (128, 16 * 8), BF16)
    weffv = sb(R, "weffv", (128, 16 * 8), BF16)
    weffa2 = sb(R, "weffa2", (128, 16 * 8), BF16)
    weffv2 = sb(R, "weffv2", (128, 16 * 8), BF16)
    bigfg_t = sb(R, "bigfg_t", (8, 2))
    atab = sb(R, "atab", (1, 64))
    mwcol = sb(R, "mwcol", (128, 2))
    identb = sb(R, "identb", (128, 128), BF16)
    g128 = sb(R, "g128", (128, 32 * NH))

    p = Prog(nc, sems, cnt)

    A = ExitStack()
    wupb = sb(A, "wupb", (128, 8 * DHID), BF16)
    qkw = sb(A, "qkw", (128, 2 * INNER), BF16)
    kvTw = sb(A, "kvTw", (128, 2 * INNER), BF16)
    igfg = sb(A, "igfg", (8, S))

    if 0 in phases:
        P0 = ExitStack()
        wslab = sb(P0, "wslab", (128, 4096))
        sgn = sb(P0, "sgn", (128, 2048))
        absw = sb(P0, "absw", (128, 2048))
        indw = sb(P0, "indw", (128, 2048), BF16)
        abssum = sb(P0, "abssum", (128, 44))
        siluc = sb(P0, "siluc", (128, 8))
        modrow = sb(P0, "modrow", (1, 512))
        mod48 = sb(P0, "mod48", (48, 128))
        wigsb = sb(P0, "wigsb", (128, 48 * 8))
        bdT = sb(P0, "bdT", (128, 3 * INNER))
        sgt = sb(P0, "sgt", (128, 8))
        sgt2 = sb(P0, "sgt2", (128, 8))
        ps_r = psum(P0, "ps_r", (128, 512))
        ps_s = psum(P0, "ps_s", (128, 512))
        ps_t = psum(P0, "ps_t", (128, 512))
        ps_q = psum(P0, "ps_q", (128, 512))
        ps_u = psum(P0, "ps_u", (128, 128))

        with nc.Block() as blk:
            for dst, src in [(onesrow, D["onesrow"]), (negonesrow, D["negonesrow"]),
                             (onescol, D["onescol"]), (ident, D["ident"]),
                             (conv1w, D["conv1w"]), (conv1b, D["conv1b"]), (normw, D["normw"]),
                             (skip_t, D["skip"]), (conv2w, D["conv2w"]), (conv2b, D["conv2b"]),
                             (siluc, D["crow"]), (bigfg_t, D["bigfg"])]:
                p.sy(lambda e, d=dst, s=src: e.dma_start(
                    out=d[:, 0:1] if d is bigfg_t else d[:], in_=s))
            p.sy(lambda e: e.dma_start(
                out=wigsb[:].rearrange("p (c n) -> p c n", c=48),
                in_=D["wig"].rearrange("(c p) n -> p c n", p=128)))
            for i, nm in enumerate(["wqbdT", "wkbdT", "wvbdT"]):
                p.sy(lambda e, i=i, nm=nm: e.dma_start(
                    out=bdT[:, i * INNER:(i + 1) * INNER].rearrange("p (c n) -> p c n", c=16),
                    in_=D[nm].rearrange("(c p) n -> p c n", p=128)))
            ld0 = cnt["ld"]
            p.g(lambda e: e.dma_start(out=onescol_b[:], in_=D["onescol"][:, 0:2]), inc="gld")
            p.g(lambda e: e.dma_start(out=identb[:], in_=D["ident"]), inc="gld")
            p.g(lambda e: e.dma_start(
                out=wupb[:].rearrange("p (c n) -> p c n", c=8),
                in_=D["wup"].rearrange("(c p) n -> p c n", p=128)), inc="gld")
            for i, nm in enumerate(["wqbd", "wkbd"]):
                p.g(lambda e, i=i, nm=nm: e.dma_start(
                    out=qkw[:, i * INNER:(i + 1) * INNER].rearrange("p (c n) -> p c n", c=16),
                    in_=D[nm].rearrange("(c p) n -> p c n", p=128)), inc="gld")
            for i, nm in enumerate(["wkbd", "wvbd"]):
                p.g(lambda e, i=i, nm=nm: e.dma_start(
                    out=kvTw[:, i * INNER:(i + 1) * INNER].rearrange("p (c n) -> p c n", c=16),
                    in_=D[nm].rearrange("(c p) n -> p c n", p=128)), inc="gld")

            p.wait("vector", "ld", ld0)
            p.v(lambda e: e.tensor_scalar(omi[:, 0:2], onescol[:, 0:2], S29, C23, ALU.mult, ALU.add), inc="dve")
            p.wait("vector", "dve")
            p.v(lambda e: e.tensor_scalar(omi[:, 2:4], omi[:, 0:2], C23, P29, ALU.subtract, ALU.mult), inc="dve")
            p.wait("vector", "dve")
            p.v(lambda e: e.tensor_sub(omi[:, 0:2], onescol[:, 0:2], omi[:, 2:4]), inc="dve")

            p.wait("scalar", "ld", ld0)
            p.s(lambda e: e.activation(siluc[:], siluc[:], AF.Silu), inc="act")
            act_siluc = cnt["act"]

            for g in range(12):
                p.guard_wait("sync", "wslab")
                p.sy(lambda e, g=g: e.dma_start(
                    out=wslab[:, 0:8 * 512].rearrange("p (c n) -> p c n", c=8),
                    in_=D["wmod"].rearrange("(c p) n -> p c n", p=128)[:, :, g * 512:(g + 1) * 512]))
                p.wait("tensor", "ld")
                if g == 0:
                    p.wait("tensor", "act", act_siluc)
                p.guard_wait("tensor", "psr_mod")
                for cc in range(8):
                    p.t(lambda e, cc=cc: e.matmul(
                        ps_r[0:1, 0:512], siluc[:, cc:cc + 1], wslab[:, cc * 512:(cc + 1) * 512],
                        start=(cc == 0), stop=(cc == 7)),
                        inc=("mm" if cc == 7 else None))
                p.guard_set("wslab", "mm")
                p.wait("vector", "mm")
                p.guard_wait("vector", "modrow")
                p.v(lambda e: e.tensor_copy(modrow[0:1, 0:512], ps_r[0:1, 0:512]), inc="dve")
                p.guard_set("psr_mod", "dve")
                p.wait("scalar", "dve")
                p.s(lambda e, g=g: e.dma_start(out=SC["modscr"][0:1, g * 512:(g + 1) * 512],
                                               in_=modrow[0:1, 0:512]), inc="st", inc_n=16)
                p.guard_set("modrow", "st")
            p.wait("sync", "st")
            p.sy(lambda e: e.dma_start(
                out=mod48[:], in_=SC["modscr"].rearrange("o (p n) -> (o p) n", p=48)))
            p.wait("tensor", "ld")
            p.t(lambda e: e.matmul(ps_u[:, 0:48], mod48[:], ident[0:48, 0:48], start=True, stop=True), inc="mm")
            p.wait("vector", "mm")
            p.v(lambda e: e.tensor_copy(modv[:], ps_u[:, 0:48]), inc="dve")
            p.wait("vector", "dve")
            p.v(lambda e: e.tensor_scalar_add(modv[:, 8:16], modv[:, 8:16], 1.0))
            p.v(lambda e: e.tensor_scalar_add(modv[:, 32:40], modv[:, 32:40], 1.0), inc="dve")
            p.wait("scalar", "dve")
            for base in (16, 40):
                p.s(lambda e, b=base: e.activation(sgt[:, 0:8], modv[:, b:b + 8], AF.Sign), inc="act")
                p.wait("scalar", "act")
                p.s(lambda e, b=base: e.activation(modv[:, b:b + 8], modv[:, b:b + 8], AF.Abs), inc="act")
                p.wait("vector", "act")
                p.v(lambda e, b=base: e.tensor_scalar_max(modv[:, b:b + 8], modv[:, b:b + 8], 0.25), inc="dve")
                p.wait("vector", "dve")
                p.v(lambda e, b=base: e.tensor_mul(modv[:, b:b + 8], modv[:, b:b + 8], sgt[:, 0:8]), inc="dve")
                p.wait("scalar", "dve")

            p.wait("tensor", "ld", ld0)
            p.wait("tensor", "dve", cnt["dve"])
            p.wait("tensor", "act", cnt["act"])
            for c in range(16):
                p.t(lambda e, c=c: e.matmul(ps_u[:, 0:8],
                                            bdT[:, c * 128:(c + 1) * 128],
                                            wigsb[:, c * 8:(c + 1) * 8], start=True, stop=False))
                p.t(lambda e, c=c: e.matmul(ps_u[:, 0:8], bdT[:, INNER + c * 128:INNER + (c + 1) * 128],
                                            wigsb[:, (16 + c) * 8:(17 + c) * 8], start=False, stop=True))
                p.t(lambda e, c=c: e.matmul(ps_u[:, 8:16], bdT[:, 2 * INNER + c * 128:2 * INNER + (c + 1) * 128],
                                            wigsb[:, (32 + c) * 8:(33 + c) * 8], start=True, stop=True), inc="mm")
                p.wait("vector", "mm")
                p.v(lambda e, c=c: e.tensor_copy(weffa[:, c * 8:(c + 1) * 8], ps_u[:, 0:8]))
                p.v(lambda e, c=c: e.tensor_copy(weffv[:, c * 8:(c + 1) * 8], ps_u[:, 8:16]), inc="dve")
                p.wait("vector", "dve")
                p.v(lambda e, c=c: e.tensor_copy(sgt[:, 0:8], weffa[:, c * 8:(c + 1) * 8]))
                p.v(lambda e, c=c: e.tensor_copy(sgt2[:, 0:8], weffv[:, c * 8:(c + 1) * 8]), inc="dve")
                p.wait("vector", "dve")
                p.v(lambda e, c=c: e.tensor_sub(sgt[:, 0:8], ps_u[:, 0:8], sgt[:, 0:8]))
                p.v(lambda e, c=c: e.tensor_sub(sgt2[:, 0:8], ps_u[:, 8:16], sgt2[:, 0:8]), inc="dve")
                p.wait("vector", "dve")
                p.v(lambda e, c=c: e.tensor_copy(weffa2[:, c * 8:(c + 1) * 8], sgt[:, 0:8]))
                p.v(lambda e, c=c: e.tensor_copy(weffv2[:, c * 8:(c + 1) * 8], sgt2[:, 0:8]), inc="dve")
                p.wait("tensor", "dve")

            subslabs = []
            co = 0
            for nm, nsl, cols, _ in [("w1T", 8, 4096, 0), ("w2T", 32, 1024, 8)]:
                for sl in range(nsl):
                    for c0_ in range(0, cols, 2048):
                        cw_ = min(2048, cols - c0_)
                        subslabs.append((nm, sl, c0_, cw_, co))
                    co += 1
            for nm, sl, c0_, cw_, co in subslabs:
                p.guard_wait("sync", "wslab")
                p.sy(lambda e, nm=nm, sl=sl, c0_=c0_, cw_=cw_: e.dma_start(
                    out=wslab[:, 0:cw_], in_=D[nm][sl * 128:(sl + 1) * 128, c0_:c0_ + cw_]))
                p.wait("vector", "ld")
                acc = (c0_ > 0)
                p.v(lambda e, co=co, cw_=cw_, acc=acc: e.reduce_sum(
                    abssum[:, co + (4 if acc else 0):co + (4 if acc else 0) + 1] if False else
                    abssum[:, co:co + 1] if not acc else abssum[:, 42:43],
                    wslab[:, 0:cw_], axis=AX.X, apply_absolute_value=True), inc="dve")
                if acc:
                    p.wait("vector", "dve")
                    p.v(lambda e, co=co: e.tensor_add(abssum[:, co:co + 1], abssum[:, co:co + 1],
                                                      abssum[:, 42:43]), inc="dve")
                p.guard_set("wslab", "dve")
            p.wait("vector", "dve")
            p.v(lambda e: e.reduce_sum(abssum[:, 40:41], abssum[:, 0:8], axis=AX.X))
            p.v(lambda e: e.reduce_sum(abssum[:, 41:42], abssum[:, 8:40], axis=AX.X), inc="dve")
            p.wait("tensor", "dve")
            p.t(lambda e: e.matmul(ps_u[0:1, 0:2], onescol[:, 0:1], abssum[:, 40:42],
                                   start=True, stop=True), inc="mm")
            p.wait("vector", "mm")
            p.v(lambda e: e.tensor_scalar(swrow[0:1, 0:2], ps_u[0:1, 0:2],
                                          1.0 / (DHID * E), 1e-5, ALU.mult, ALU.max), inc="dve")
            p.wait("vector", "dve")
            p.v(lambda e: e.tensor_scalar_mul(swrow[0:1, 2:4], swrow[0:1, 0:2], 0.5), inc="dve")
            p.wait("tensor", "dve")
            p.t(lambda e: e.matmul(ps_u[:, 0:2], onesrow[:], swrow[0:1, 2:4], start=True, stop=True), inc="mm")
            p.wait("vector", "mm")
            p.v(lambda e: e.tensor_copy(thrcol[:, 0:2], ps_u[:, 0:2]), inc="dve")
            p.wait("tensor", "dve")
            p.t(lambda e: e.matmul(ps_u[:, 0:2], onesrow[:], swrow[0:1, 0:2], start=True, stop=True), inc="mm")
            p.wait("vector", "mm")
            p.v(lambda e: e.tensor_copy(mwcol[:, 0:2], ps_u[:, 0:2]), inc="dve")
            p.wait("scalar", "dve")
            p.wait("vector", "dve")
            for nm, sl, c0_, cw_, co in subslabs:
                wi = 0 if nm == "w1T" else 1
                dst = SC["w1q_s"] if wi == 0 else SC["w2q_s"]
                p.guard_wait("sync", "wslab")
                p.sy(lambda e, nm=nm, sl=sl, c0_=c0_, cw_=cw_: e.dma_start(
                    out=wslab[:, 0:cw_], in_=D[nm][sl * 128:(sl + 1) * 128, c0_:c0_ + cw_]))
                ldh = cnt["ld"]
                p.wait("scalar", "ld", ldh)
                p.guard_wait("scalar", "sgnbuf")
                p.guard_wait("scalar", "abswbuf")
                p.s(lambda e, cw_=cw_: e.activation(sgn[:, 0:cw_], wslab[:, 0:cw_], AF.Sign))
                p.s(lambda e, cw_=cw_: e.activation(absw[:, 0:cw_], wslab[:, 0:cw_], AF.Abs), inc="act")
                p.guard_set("wslab", "act")
                p.wait("vector", "act")
                p.guard_wait("vector", "indbuf")
                p.v(lambda e, wi=wi, cw_=cw_: e.tensor_scalar(
                    indw[:, 0:cw_], absw[:, 0:cw_], thrcol[:, wi:wi + 1], None, ALU.is_gt), inc="dve")
                p.wait("vector", "dve")
                p.v(lambda e, cw_=cw_: e.tensor_mul(absw[:, 0:cw_], sgn[:, 0:cw_], indw[:, 0:cw_]), inc="dve")
                p.guard_set("sgnbuf", "dve")
                p.guard_set("indbuf", "dve")
                p.wait("gpsimd", "dve")
                p.g(lambda e, dst=dst, sl=sl, c0_=c0_, cw_=cw_: e.dma_start(
                    out=dst[sl * 128:(sl + 1) * 128, c0_:c0_ + cw_], in_=absw[:, 0:cw_]), inc="st")
                p.guard_set("abswbuf", "gst")
            if debug:
                p.wait("gpsimd", "dve")
                p.g(lambda e: e.dma_start(out=dbg["d_mod"], in_=modv[:]), inc="st")
                p.g(lambda e: e.dma_start(out=dbg["d_sw"], in_=swrow[:]), inc="st")
                p.g(lambda e: e.dma_start(out=dbg["d_omi"], in_=omi[:]), inc="st")
                for wi in range(2):
                    src_ = SC["w1q_s"] if wi == 0 else SC["w2q_s"]
                    dd = dbg["d_w1q"] if wi == 0 else dbg["d_w2q"]
                    p.wait("gpsimd", "gst")
                    p.g(lambda e, s_=src_, d_=dd: e.dma_start(out=d_, in_=s_), inc="st")
            p.wait("gpsimd", "gst")
            p.flush(blk)
        P0.close()

    if 1 in phases:
        P1 = ExitStack()
        xtb = sb(P1, "xtb", (128, 8 * TP1), BF16)
        sq1 = sb(P1, "sq1", (128, TP1))
        sq2 = sb(P1, "sq2", (128, TP1))
        rows1 = sb(P1, "rows1", (1, 3 * TP1))
        hb = sb(P1, "hb", (128, 8 * TP1), BF16)
        xm = sb(P1, "xm", (128, 16 * (TP1 + 3)), BF16)
        xact = sb(P1, "xact", (128, 16 * TP1), BF16)
        cacc = sb(P1, "cacc", (128, TP1))
        cacc2 = sb(P1, "cacc2", (128, TP1))
        szst = sb(P1, "szst", (128, 4 * TP1), BF16)
        qkst = sb(P1, "qkst", (128, 4 * TP1), BF16)
        tvst = sb(P1, "tvst", (128, 4 * DH), BF16)
        ps_a = psum(P1, "ps_a", (128, 512))
        ps_b = psum(P1, "ps_b", (128, 512))
        ps_c = psum(P1, "ps_c", (128, 512))
        ps_d = psum(P1, "ps_d", (128, 512))
        ps_rows = psum(P1, "ps_rows", (128, 512))
        ps_bc1 = psum(P1, "ps_bc1", (128, 512))
        ps_bc2 = psum(P1, "ps_bc2", (128, 512))
        ps_ig = psum(P1, "ps_ig", (8, 512))

        with nc.Block() as blk:
            p.wait("tensor", "gld")
            p.wait("tensor", "dve")
            for t in range(NT1):
                t0 = t * TP1
                p.guard_wait("gpsimd", "xtb")
                p.g(lambda e, t0=t0: e.dma_start(
                    out=xtb[:].rearrange("p (c n) -> p c n", c=8),
                    in_=D["xT"].rearrange("(c p) n -> p c n", p=128)[:, :, t0:t0 + TP1]), inc="gld")
                gld_x = cnt["gld"]
                p.wait("tensor", "gld", gld_x)
                for c in range(8):
                    p.t(lambda e, c=c: e.matmul(
                        ps_rows[0:1, 0:TP1], onescol_b[:, 0:1], xtb[:, c * TP1:(c + 1) * TP1],
                        start=(c == 0), stop=(c == 7)), inc=("mm" if c == 7 else None))
                p.wait("scalar", "gld", gld_x)
                for c in range(8):
                    d_ = sq1 if c % 2 == 0 else sq2
                    if c >= 2:
                        p.wait("scalar", "mm", cnt["mm"])
                    p.s(lambda e, c=c, d=d_: e.activation(d[:], xtb[:, c * TP1:(c + 1) * TP1], AF.Square), inc="act")
                    p.wait("tensor", "act")
                    p.t(lambda e, c=c, d=d_: e.matmul(
                        ps_rows[32:33, 0:TP1], onescol[:, 0:1], d[:],
                        start=(c == 0), stop=(c == 7)), inc="mm")
                p.wait("vector", "mm")
                s0 = rows1[0:1, 0:TP1]; s1 = rows1[0:1, TP1:2 * TP1]; s2 = rows1[0:1, 2 * TP1:3 * TP1]
                p.v(lambda e, s0=s0: e.tensor_scalar_mul(s0, ps_rows[0:1, 0:TP1], 1.0 / E))
                p.v(lambda e, s1=s1: e.tensor_scalar_mul(s1, ps_rows[32:33, 0:TP1], 1.0 / E), inc="dve")
                p.wait("vector", "dve")
                p.v(lambda e, s0=s0, s2=s2: e.tensor_mul(s2, s0, s0), inc="dve")
                p.wait("vector", "dve")
                p.v(lambda e, s1=s1, s2=s2: e.tensor_sub(s1, s1, s2), inc="dve")
                p.wait("vector", "dve")
                p.v(lambda e, s1=s1: e.tensor_scalar_add(s1, s1, 1e-5), inc="dve")
                p.wait("scalar", "dve")
                p.s(lambda e, s1=s1, s2=s2: e.activation(s2, s1, AF.Sqrt), inc="act")
                p.wait("vector", "act")
                p.v(lambda e, s1=s1, s2=s2: e.reciprocal(s1, s2), inc="dve")
                p.wait("vector", "dve")
                p.v(lambda e, s0=s0, s1=s1, s2=s2: e.tensor_mul(s2, s0, s1), inc="dve")
                p.wait("vector", "dve")
                p.v(lambda e, s2=s2: e.tensor_scalar_mul(s2, s2, -1.0), inc="dve")
                p.wait("tensor", "dve")
                p.t(lambda e, s1=s1: e.matmul(ps_bc1[:, 0:TP1], onesrow[:], s1, start=True, stop=True))
                p.t(lambda e, s2=s2: e.matmul(ps_bc2[:, 0:TP1], onesrow[:], s2, start=True, stop=True), inc="mm")
                p.wait("vector", "mm")
                p.guard_wait("vector", "hb")
                for c in range(8):
                    p.v(lambda e, c=c: e.tensor_mul(sq1[:], xtb[:, c * TP1:(c + 1) * TP1], ps_bc1[:, 0:TP1]))
                    p.v(lambda e, c=c: e.tensor_add(sq1[:], sq1[:], ps_bc2[:, 0:TP1]))
                    p.v(lambda e, c=c: e.tensor_scalar(
                        hb[:, c * TP1:(c + 1) * TP1], sq1[:], modv[:, 8 + c:9 + c], modv[:, 0 + c:1 + c],
                        ALU.mult, ALU.add), inc=("dve" if c == 7 else None))
                hb_done = cnt["dve"]
                p.wait("tensor", "dve", hb_done)
                for ic in range(32):
                    pb = ps_a if ic % 2 == 0 else ps_b
                    p.guard_wait("tensor", f"wubank{ic % 2}")
                    for kc in range(8):
                        p.t(lambda e, ic=ic, kc=kc, pb=pb: e.matmul(
                            pb[:, 0:TP1], wupb[:, kc * DHID + ic * 128:kc * DHID + (ic + 1) * 128],
                            hb[:, kc * TP1:(kc + 1) * TP1],
                            start=(kc == 0), stop=(kc == 7)), inc=("mm" if kc == 7 else None))
                    mmv = cnt["mm"]
                    if ic < 16:
                        p.wait("vector", "mm", mmv)
                        if t > 0 and ic == 0:
                            p.guard_wait("vector", "xmtile")
                        p.v(lambda e, ic=ic, pb=pb: e.tensor_copy(
                            xm[:, ic * (TP1 + 3) + 3:(ic + 1) * (TP1 + 3)], pb[:, 0:TP1]), inc="dve")
                        p.guard_set(f"wubank{ic % 2}", "dve")
                    else:
                        sl = ic % 4
                        p.wait("scalar", "mm", mmv)
                        if t > 0 or ic >= 20:
                            p.guard_wait("scalar", f"szst{sl}")
                        p.s(lambda e, sl=sl, pb=pb: e.activation(
                            szst[:, sl * TP1:(sl + 1) * TP1], pb[:, 0:TP1], AF.Silu), inc="act")
                        p.guard_set(f"wubank{ic % 2}", "act")
                        p.wait("gpsimd", "act")
                        p.g(lambda e, sl=sl, ic=ic, t0=t0: e.dma_start(
                            out=SC["sz_s"][(ic - 16) * 128:(ic - 15) * 128, t0:t0 + TP1],
                            in_=szst[:, sl * TP1:(sl + 1) * TP1]), inc="st")
                        p.guard_set(f"szst{sl}", "gst")
                if t == 0:
                    for c in range(16):
                        b = c * (TP1 + 3)
                        p.v(lambda e, b=b: e.tensor_scalar_mul(xm[:, b:b + 3], xm[:, b:b + 3] if False else onescol[:, 0:3], 0.0),
                            inc=("dve" if c == 15 else None))
                    p.wait("vector", "dve")
                p.guard_wait("scalar", "xact")
                p.guard_wait("scalar", "xactdma")
                for c in range(16):
                    b = c * (TP1 + 3)
                    cc_ = cacc[:, 0:TP1] if c % 2 == 0 else cacc2[:, 0:TP1]
                    p.guard_wait("vector", f"cacc{c % 2}")
                    p.v(lambda e, c=c, b=b, cc_=cc_: e.tensor_scalar(
                        cc_, xm[:, b:b + TP1], conv1w[:, 0 * 16 + c:0 * 16 + c + 1], None, ALU.mult))
                    for tp_ in range(1, 4):
                        p.v(lambda e, c=c, b=b, tp_=tp_, cc_=cc_: e.scalar_tensor_tensor(
                            cc_, xm[:, b + tp_:b + tp_ + TP1], conv1w[:, tp_ * 16 + c:tp_ * 16 + c + 1],
                            cc_, ALU.mult, ALU.add), inc=("dve" if tp_ == 3 else None))
                    p.wait("scalar", "dve")
                    p.s(lambda e, c=c, cc_=cc_: e.activation(
                        xact[:, c * TP1:(c + 1) * TP1], cc_, AF.Silu,
                        bias=conv1b[:, c:c + 1], scale=1.0), inc="act")
                    p.guard_set(f"cacc{c % 2}", "act")
                xact_done = cnt["act"]
                if t < NT1 - 1:
                    p.wait("vector", "act", xact_done)
                    for c in range(16):
                        b = c * (TP1 + 3)
                        p.v(lambda e, b=b: e.tensor_copy(xm[:, b:b + 3], xm[:, b + TP1:b + TP1 + 3]),
                            inc=("dve" if c == 15 else None))
                    p.guard_set("xmtile", "dve")
                p.wait("gpsimd", "act", xact_done)
                p.g(lambda e, t0=t0: e.dma_start(
                    out=SC["xact_s"].rearrange("(c p) n -> p c n", p=128)[:, :, t0:t0 + TP1],
                    in_=xact[:].rearrange("p (c n) -> p c n", c=16)), inc="st")
                
                p.wait("tensor", "act", xact_done)
                for c in range(16):
                    for iw, dst_s in ((0, SC["q_s"]), (1, SC["k_s"])):
                        pb = ps_c if (2 * c + iw) % 2 == 0 else ps_d
                        sl = (2 * c + iw) % 4
                        p.guard_wait("tensor", f"qkbank{(2 * c + iw) % 2}")
                        p.t(lambda e, c=c, iw=iw, pb=pb: e.matmul(
                            pb[:, 0:TP1], qkw[:, iw * INNER + c * 128:iw * INNER + (c + 1) * 128],
                            xact[:, c * TP1:(c + 1) * TP1], start=True, stop=True), inc="mm")
                        p.wait("vector", "mm")
                        p.guard_wait("vector", f"qkst{sl}")
                        p.v(lambda e, sl=sl, pb=pb: e.tensor_copy(qkst[:, sl * TP1:(sl + 1) * TP1], pb[:, 0:TP1]), inc="dve")
                        p.guard_set(f"qkbank{(2 * c + iw) % 2}", "dve")
                        p.wait("gpsimd", "dve")
                        p.g(lambda e, sl=sl, c=c, t0=t0, dst_s=dst_s: e.dma_start(
                            out=dst_s[c * 128:(c + 1) * 128, t0:t0 + TP1],
                            in_=qkst[:, sl * TP1:(sl + 1) * TP1]), inc="st")
                        p.guard_set(f"qkst{sl}", "gst")
                for ts_ in range(TP1 // 128):
                    for hd in range(NH):
                        for iw, src_t, dst_s in ((0, xact, SC["ktok_s"]), (1, xm, SC["vtok_s"])):
                            pb = ps_c if iw == 0 else ps_d
                            sl = (ts_ * NH + hd) % 2 * 2 + iw
                            p.guard_wait("tensor", f"qkbank{iw}")
                            for fc in range(4):
                                c = hd * 4 + fc
                                p.t(lambda e, c=c, ts_=ts_, fc=fc, iw=iw, pb=pb: e.matmul(
                                    pb[:, fc * 128:(fc + 1) * 128],
                                    (xact[:, c * TP1 + ts_ * 128:c * TP1 + (ts_ + 1) * 128] if iw == 0
                                     else xm[:, c * (TP1 + 3) + 3 + ts_ * 128:c * (TP1 + 3) + 3 + (ts_ + 1) * 128]),
                                    kvTw[:, iw * INNER + c * 128:iw * INNER + (c + 1) * 128],
                                    start=True, stop=True), inc=("mm" if fc == 3 else None))
                            p.wait("vector", "mm")
                            p.guard_wait("vector", f"tvst{sl}")
                            p.v(lambda e, sl=sl, pb=pb: e.tensor_copy(tvst[:, sl * DH:(sl + 1) * DH], pb[:, 0:DH]), inc="dve")
                            p.guard_set(f"qkbank{iw}", "dve")
                            p.wait("gpsimd", "dve")
                            p.g(lambda e, sl=sl, hd=hd, t0=t0, ts_=ts_, dst_s=dst_s: e.dma_start(
                                out=dst_s[hd * S + t0 + ts_ * 128:hd * S + t0 + (ts_ + 1) * 128, :],
                                in_=tvst[:, sl * DH:(sl + 1) * DH]), inc="st")
                            p.guard_set(f"tvst{sl}", "gst")
                p.guard_wait("tensor", "igps")
                for c in range(16):
                    p.t(lambda e, c=c: e.matmul(
                        ps_ig[0:8, 0:TP1], weffa[:, c * 8:(c + 1) * 8], xact[:, c * TP1:(c + 1) * TP1],
                        start=(c == 0), stop=False))
                    p.t(lambda e, c=c: e.matmul(
                        ps_ig[0:8, 0:TP1], weffa2[:, c * 8:(c + 1) * 8], xact[:, c * TP1:(c + 1) * TP1],
                        start=False, stop=False))
                for c in range(16):
                    p.t(lambda e, c=c: e.matmul(
                        ps_ig[0:8, 0:TP1], weffv[:, c * 8:(c + 1) * 8],
                        xm[:, c * (TP1 + 3) + 3:(c + 1) * (TP1 + 3)],
                        start=False, stop=False))
                    p.t(lambda e, c=c: e.matmul(
                        ps_ig[0:8, 0:TP1], weffv2[:, c * 8:(c + 1) * 8],
                        xm[:, c * (TP1 + 3) + 3:(c + 1) * (TP1 + 3)],
                        start=False, stop=(c == 15)), inc=("mm" if c == 15 else None))
                p.wait("vector", "mm")
                p.v(lambda e, t0=t0: e.tensor_scalar(
                    igfg[:, t0:t0 + TP1], ps_ig[0:8, 0:TP1], bigfg_t[0:8, 0:1], None, ALU.add), inc="dve")
                p.guard_set("igps", "dve")
                p.guard_set("xtb", "dve")
                p.guard_set("hb", "mm")
                p.guard_set("xact", "mm")
            if debug:
                p.wait("gpsimd", "dve")
                p.wait("gpsimd", "gst")
                p.g(lambda e: e.dma_start(out=dbg["d_igfg"], in_=igfg[:]), inc="st")
                p.g(lambda e: e.dma_start(out=dbg["d_xact"], in_=SC["xact_s"]), inc="st")
                p.g(lambda e: e.dma_start(out=dbg["d_q"], in_=SC["q_s"]), inc="st")
                p.g(lambda e: e.dma_start(out=dbg["d_k"], in_=SC["k_s"]), inc="st")
                p.g(lambda e: e.dma_start(out=dbg["d_sz"], in_=SC["sz_s"]), inc="st")
                p.g(lambda e: e.dma_start(out=dbg["d_vtok"], in_=SC["vtok_s"]), inc="st")
                p.wait("gpsimd", "gst")
            p.flush(blk)
        P1.close()

    if 1 in phases:
        P1b = ExitStack()
        fg4 = sb(P1b, "fg4", (NH, S))
        mrow = sb(P1b, "mrow", (NH, S))
        espm = sb(P1b, "espm", (NH, S))
        sprow = sb(P1b, "sprow", (NH, S))
        sprow2 = sb(P1b, "sprow2", (NH, S))
        grow_t = sb(P1b, "grow_t", (NH, S))
        g32 = sb(P1b, "g32", (32, 128))
        ps_g = psum(P1b, "ps_g", (128, 512))
        with nc.Block() as blk:
            p.wait("sync", "dve")
            p.sy(lambda e: e.dma_start(out=fg4[:], in_=igfg[4:8, :]))
            p.wait("scalar", "ld")
            p.s(lambda e: e.activation(sprow[:], fg4[:], AF.Exp, scale=-1.0), inc="act")
            p.wait("scalar", "act")
            p.s(lambda e: e.activation(sprow[:], sprow[:], AF.Ln, bias=1.0), inc="act")
            p.wait("vector", "act")
            p.v(lambda e: e.tensor_tensor_scan(sprow2[:], sprow[:], sprow[:], 0.0, ALU.add, ALU.bypass), inc="dve")
            p.wait("vector", "dve")
            p.v(lambda e: e.tensor_add(grow_t[:], igfg[0:4, :], sprow2[:]), inc="dve")
            p.wait("vector", "dve")
            p.v(lambda e: e.tensor_tensor_scan(mrow[:], grow_t[:], grow_t[:], -1e30, ALU.max, ALU.bypass), inc="dve")
            p.wait("vector", "dve")
            p.v(lambda e: e.tensor_sub(espm[:], sprow2[:], mrow[:]), inc="dve")
            p.wait("scalar", "dve")
            p.s(lambda e: e.activation(espm[:], espm[:], AF.Exp), inc="act")
            p.wait("vector", "dve")
            p.v(lambda e: e.tensor_scalar_add(grow_t[:], grow_t[:], LNDH), inc="dve")
            p.wait("scalar", "dve")
            if debug:
                p.wait("gpsimd", "act")
                p.g(lambda e: e.dma_start(out=dbg["d_G"], in_=grow_t[:]), inc="st")
                p.g(lambda e: e.dma_start(out=dbg["d_M"], in_=mrow[:]), inc="st")
                p.g(lambda e: e.dma_start(out=dbg["d_espm"], in_=espm[:]), inc="st")
            for hd in range(NH):
                p.guard_wait("scalar", "modscr")
                p.s(lambda e, hd=hd: e.dma_start(out=SC["modscr"][0:1, 0:4096], in_=grow_t[hd:hd + 1, :]),
                    inc="st", inc_n=16)
                p.wait("sync", "st")
                p.guard_wait("sync", "g32")
                p.sy(lambda e: e.dma_start(
                    out=g32[:], in_=SC["modscr"][0:1, 0:4096].rearrange("o (p n) -> (o p) n", p=32)))
                p.guard_set("modscr", "ld")
                p.wait("tensor", "ld")
                if hd == 0:
                    p.wait("tensor", "act")
                p.t(lambda e: e.matmul(ps_g[:, 0:32], g32[:], ident[0:32, 0:32], start=True, stop=True), inc="mm")
                p.guard_set("g32", "mm")
                p.wait("vector", "mm")
                p.v(lambda e, hd=hd: e.tensor_copy(g128[:, hd * 32:(hd + 1) * 32], ps_g[:, 0:32]), inc="dve")
                p.wait("tensor", "dve")
            p.wait("gpsimd", "dve")
            p.wait("gpsimd", "act")
            p.g(lambda e: e.dma_start(out=SC["gates_s"][0:4, :], in_=mrow[:]), inc="st")
            p.g(lambda e: e.dma_start(out=SC["gates_s"][4:8, :], in_=espm[:]), inc="st")
            for hd in range(NH):
                p.g(lambda e, hd=hd: e.dma_start(
                    out=atab[0:1, hd * 8:hd * 8 + 8],
                    in_=mrow[hd:hd + 1, :].rearrange("o (i r) -> o i r", r=SG)[:, :, 0:1],
                    allow_slow_non_contiguous=True), inc="st")
            p.wait("gpsimd", "gst")
            p.flush(blk)
        P1b.close()
    A.close()

    if 2 in phases:
        P2 = ExitStack()
        masksb = sb(P2, "masksb", (128, 4 * SG), BF16)
        qbuf = sb(P2, "qbuf", (128, 4 * S), BF16)
        kbuf = sb(P2, "kbuf", (128, 4 * S), BF16)
        vbuf = sb(P2, "vbuf", (128, NKT * 512), BF16)
        xabuf = sb(P2, "xabuf", (128, 4 * SG), BF16)
        szbuf = sb(P2, "szbuf", (128, 4 * SG), BF16)
        st1 = sb(P2, "st1", (128, SG), BF16)
        st2 = sb(P2, "st2", (128, SG), BF16)
        ei = sb(P2, "ei", (128, 32))
        htr = sb(P2, "htr", (128, 4 * SG), BF16)
        sqh = sb(P2, "sqh", (128, SG))
        sqh2 = sb(P2, "sqh2", (128, SG))
        nfb = sb(P2, "nfb", (128, SG))
        mub = sb(P2, "mub", (128, SG), BF16)
        r1b = sb(P2, "r1b", (128, SG), BF16)
        rowsp = sb(P2, "rowsp", (1, 4 * SG))
        mslice = sb(P2, "mslice", (1, SG))
        nega = sb(P2, "nega", (128, 2))
        eslice = sb(P2, "eslice", (1, SG))
        hstg = sb(P2, "hstg", (128, 4 * SG), BF16)
        ps_S0 = psum(P2, "ps_S0", (128, 512))
        ps_S1 = psum(P2, "ps_S1", (128, 512))
        ps_h0 = psum(P2, "ps_h0", (128, 512))
        ps_h1 = psum(P2, "ps_h1", (128, 512))
        ps_h2 = psum(P2, "ps_h2", (128, 512))
        ps_h3 = psum(P2, "ps_h3", (128, 512))
        ps_rs = psum(P2, "ps_rs", (128, 512))
        ps_bc = psum(P2, "ps_bc", (128, 512))
        HPS = [ps_h0, ps_h1, ps_h2, ps_h3]

        with nc.Block() as blk:
            p.wait("sync", "gst", cnt["gst"])
            p.wait("tensor", "gst", cnt["gst"])
            p.sy(lambda e: e.dma_start(out=masksb[:], in_=D["masks"]))
            for hd in range(NH):
                p.guard_wait("sync", "p2bufs")
                p.sy(lambda e, hd=hd: e.dma_start(
                    out=qbuf[:].rearrange("p (c n) -> p c n", c=4),
                    in_=SC["q_s"].rearrange("(c p) n -> p c n", p=128)[:, hd * 4:hd * 4 + 4, :]))
                p.sy(lambda e, hd=hd: e.dma_start(
                    out=kbuf[:].rearrange("p (c n) -> p c n", c=4),
                    in_=SC["k_s"].rearrange("(c p) n -> p c n", p=128)[:, hd * 4:hd * 4 + 4, :]))
                p.sy(lambda e, hd=hd: e.dma_start(
                    out=vbuf[:].rearrange("p (c n) -> p c n", c=NKT),
                    in_=SC["vtok_s"].rearrange("(c p) n -> p c n", p=128)[:, hd * NKT:(hd + 1) * NKT, :]))
                ld_head = cnt["ld"]
                for i in range(NSG):
                    s0 = i * SG
                    njt = 4 * i + 4
                    p.wait("tensor", "ld", ld_head)
                    p.guard_wait("tensor", "psbc")
                    p.t(lambda e, hd=hd, i=i: e.matmul(
                        ps_bc[:, 0:1], negonesrow[:], atab[0:1, hd * 8 + i:hd * 8 + i + 1],
                        start=True, stop=True), inc="mm")
                    p.wait("vector", "mm")
                    p.guard_wait("vector", "negabuf")
                    p.v(lambda e: e.tensor_copy(nega[:, 0:1], ps_bc[:, 0:1]), inc="dve")
                    p.guard_set("psbc", "dve")
                    p.wait("scalar", "dve")
                    p.guard_wait("scalar", "eibuf")
                    p.s(lambda e, hd=hd, njt=njt: e.activation(
                        ei[:, 0:njt], g128[:, hd * 32:hd * 32 + njt], AF.Exp,
                        bias=nega[:, 0:1], scale=1.0), inc="act")
                    p.guard_set("negabuf", "act")
                    ei_ready = cnt["act"]
                    p.guard_wait("sync", "gsl")
                    p.sy(lambda e, hd=hd, s0=s0: e.dma_start(
                        out=mslice[:], in_=SC["gates_s"][hd:hd + 1, s0:s0 + SG]))
                    p.sy(lambda e, hd=hd, s0=s0: e.dma_start(
                        out=eslice[:], in_=SC["gates_s"][4 + hd:5 + hd, s0:s0 + SG]))
                    p.guard_wait("sync", "xaszbuf")
                    p.sy(lambda e, hd=hd, s0=s0: e.dma_start(
                        out=xabuf[:].rearrange("p (c n) -> p c n", c=4),
                        in_=SC["xact_s"].rearrange("(c p) n -> p c n", p=128)[:, hd * 4:hd * 4 + 4, s0:s0 + SG]))
                    p.sy(lambda e, hd=hd, s0=s0: e.dma_start(
                        out=szbuf[:].rearrange("p (c n) -> p c n", c=4),
                        in_=SC["sz_s"].rearrange("(c p) n -> p c n", p=128)[:, hd * 4:hd * 4 + 4, s0:s0 + SG]))
                    ld_sg = cnt["ld"]
                    p.guard_wait("tensor", "hps")
                    p.guard_wait("tensor", "psrs")
                    ev_s = {}
                    for j in range(njt + 1):
                        if j < njt:
                            sps = ps_S0 if j % 2 == 0 else ps_S1
                            sbb = st1 if j % 2 == 0 else st2
                            p.guard_wait("tensor", f"sps{j % 2}")
                            for dc in range(4):
                                p.t(lambda e, hd=hd, i=i, j=j, dc=dc, sps=sps: e.matmul(
                                    sps[:, 0:SG], kbuf[:, dc * S + j * 128:dc * S + (j + 1) * 128],
                                    qbuf[:, dc * S + i * SG:dc * S + (i + 1) * SG],
                                    start=(dc == 0), stop=(dc == 3)), inc=("mm" if dc == 3 else None))
                            mm_s = cnt["mm"]
                            if j >= 4 * i:
                                p.wait("vector", "mm", mm_s)
                                if i == 0 and j == 0:
                                    p.wait("vector", "act", ei_ready)
                                p.guard_wait("vector", f"sbb{j % 2}")
                                r = j - 4 * i
                                p.v(lambda e, j=j, r=r, sps=sps, sbb=sbb: e.scalar_tensor_tensor(
                                    sbb[:], sps[:, 0:SG], ei[:, j:j + 1], masksb[:, r * SG:(r + 1) * SG],
                                    ALU.mult, ALU.mult), inc="dve")
                                ev_s[j] = ("dve", cnt["dve"])
                                p.guard_set(f"sps{j % 2}", "dve")
                            else:
                                p.wait("scalar", "mm", mm_s)
                                if i == 0 and j == 0:
                                    p.wait("scalar", "act", ei_ready)
                                p.guard_wait("scalar", f"sbb{j % 2}")
                                p.s(lambda e, j=j, sps=sps, sbb=sbb: e.activation(
                                    sbb[:], sps[:, 0:SG], AF.Copy, scale=ei[:, j:j + 1]), inc="act")
                                ev_s[j] = ("act", cnt["act"])
                                p.guard_set(f"sps{j % 2}", "act")
                        jj = j - 1
                        if jj >= 0:
                            sbp = st1 if jj % 2 == 0 else st2
                            sem_n, sem_v = ev_s.pop(jj)
                            p.wait("tensor", sem_n, sem_v)
                            for dc in range(4):
                                p.t(lambda e, jj=jj, dc=dc, sbp=sbp, njt=njt: e.matmul(
                                    HPS[dc][:, 0:SG], vbuf[:, jj * 512 + dc * 128:jj * 512 + (dc + 1) * 128],
                                    sbp[:], start=(jj == 0), stop=(jj == njt - 1)))
                            p.t(lambda e, jj=jj, sbp=sbp, njt=njt: e.matmul(
                                ps_rs[0:1, 0:SG], onescol_b[:, 0:1], sbp[:],
                                start=(jj == 0), stop=(jj == njt - 1)), inc=("mm" if jj == njt - 1 else None))
                            p.guard_set(f"sbb{jj % 2}", "mm")
                    mm_head = cnt["mm"]
                    ro0 = rowsp[0:1, 0:SG]
                    ro1 = rowsp[0:1, SG:2 * SG]
                    ro2 = rowsp[0:1, 2 * SG:3 * SG]
                    ro3 = rowsp[0:1, 3 * SG:4 * SG]
                    p.wait("scalar", "mm", mm_head)
                    p.s(lambda e, ro0=ro0: e.activation(ro0, ps_rs[0:1, 0:SG], AF.Abs), inc="act")
                    p.s(lambda e, ro3=ro3, hd=hd, i=i: e.activation(
                        ro3, mslice[:], AF.Exp, scale=-1.0,
                        bias=atab[0:1, hd * 8 + i:hd * 8 + i + 1]), inc="act")
                    p.wait("vector", "act")
                    p.wait("vector", "ld", ld_sg)
                    p.v(lambda e, ro0=ro0, ro3=ro3: e.tensor_mul(ro0, ro0, ro3), inc="dve")
                    p.wait("vector", "dve")
                    p.v(lambda e, ro0=ro0, ro1=ro1: e.tensor_max(ro1, ro0, eslice[:]), inc="dve")
                    p.wait("vector", "dve")
                    p.v(lambda e, ro1=ro1: e.tensor_scalar_add(ro1, ro1, 1e-6), inc="dve")
                    p.wait("vector", "dve")
                    p.v(lambda e, ro1=ro1, ro2=ro2: e.reciprocal(ro2, ro1), inc="dve")
                    p.wait("vector", "dve")
                    p.v(lambda e, ro2=ro2, ro3=ro3: e.tensor_mul(ro2, ro2, ro3), inc="dve")
                    p.wait("tensor", "dve")
                    p.guard_wait("tensor", "psbc")
                    p.t(lambda e, ro2=ro2: e.matmul(ps_bc[:, 0:SG], onesrow[:], ro2, start=True, stop=True), inc="mm")
                    p.wait("vector", "mm")
                    p.v(lambda e: e.tensor_copy(nfb[:], ps_bc[:, 0:SG]), inc="dve")
                    p.guard_set("psbc", "dve")
                    nfb_ready = cnt["dve"]
                    p.wait("vector", "dve", nfb_ready)
                    p.guard_wait("vector", "htrbuf")
                    for dc in range(4):
                        p.v(lambda e, dc=dc: e.tensor_mul(
                            htr[:, dc * SG:(dc + 1) * SG], HPS[dc][:, 0:SG], nfb[:]),
                            inc=("dve" if dc == 3 else None))
                    htr_ready = cnt["dve"]
                    p.guard_set("hps", "dve")
                    p.wait("tensor", "dve", htr_ready)
                    for dc in range(4):
                        p.t(lambda e, dc=dc: e.matmul(
                            ps_rs[0:1, 0:SG], onescol_b[:, 0:1], htr[:, dc * SG:(dc + 1) * SG],
                            start=(dc == 0), stop=(dc == 3)), inc=("mm" if dc == 3 else None))
                    p.wait("scalar", "dve", htr_ready)
                    for dc in range(4):
                        sq_ = sqh if dc % 2 == 0 else sqh2
                        if dc >= 2:
                            p.wait("scalar", "mm", cnt["mm"])
                        p.s(lambda e, dc=dc, sq_=sq_: e.activation(
                            sq_[:], htr[:, dc * SG:(dc + 1) * SG], AF.Square), inc="act")
                        p.wait("tensor", "act")
                        p.t(lambda e, dc=dc, sq_=sq_: e.matmul(
                            ps_rs[32:33, 0:SG], onescol[:, 0:1], sq_[:],
                            start=(dc == 0), stop=(dc == 3)), inc="mm")
                    p.wait("vector", "mm")
                    p.v(lambda e, ro0=ro0: e.tensor_scalar_mul(ro0, ps_rs[0:1, 0:SG], 1.0 / DH))
                    p.v(lambda e, ro1=ro1: e.tensor_scalar_mul(ro1, ps_rs[32:33, 0:SG], 1.0 / DH), inc="dve")
                    p.guard_set("psrs", "dve")
                    p.wait("vector", "dve")
                    p.v(lambda e, ro0=ro0, ro2=ro2: e.tensor_mul(ro2, ro0, ro0), inc="dve")
                    p.wait("vector", "dve")
                    p.v(lambda e, ro1=ro1, ro2=ro2: e.tensor_sub(ro1, ro1, ro2), inc="dve")
                    p.wait("vector", "dve")
                    p.v(lambda e, ro1=ro1: e.tensor_scalar_add(ro1, ro1, 1e-6), inc="dve")
                    p.wait("scalar", "dve")
                    p.s(lambda e, ro1=ro1, ro2=ro2: e.activation(ro2, ro1, AF.Sqrt), inc="act")
                    p.wait("vector", "act")
                    p.v(lambda e, ro1=ro1, ro2=ro2: e.reciprocal(ro1, ro2), inc="dve")
                    p.wait("tensor", "dve")
                    p.guard_wait("tensor", "psbc")
                    p.t(lambda e, ro0=ro0: e.matmul(ps_bc[:, 0:SG], onesrow[:], ro0, start=True, stop=True), inc="mm")
                    p.wait("vector", "mm")
                    p.v(lambda e: e.tensor_copy(mub[:], ps_bc[:, 0:SG]), inc="dve")
                    p.wait("tensor", "dve")
                    p.t(lambda e, ro1=ro1: e.matmul(ps_bc[:, 0:SG], onesrow[:], ro1, start=True, stop=True), inc="mm")
                    p.wait("vector", "mm")
                    p.v(lambda e: e.tensor_copy(r1b[:], ps_bc[:, 0:SG]), inc="dve")
                    p.guard_set("psbc", "dve")
                    p.wait("vector", "dve")
                    p.wait("vector", "ld", ld_sg)
                    p.guard_wait("vector", "hstg")
                    for dc in range(4):
                        cix = hd * 4 + dc
                        p.v(lambda e, dc=dc: e.tensor_sub(
                            htr[:, dc * SG:(dc + 1) * SG], htr[:, dc * SG:(dc + 1) * SG], mub[:]))
                        p.v(lambda e, dc=dc, cix=cix: e.scalar_tensor_tensor(
                            htr[:, dc * SG:(dc + 1) * SG], htr[:, dc * SG:(dc + 1) * SG],
                            normw[:, cix:cix + 1], r1b[:], ALU.mult, ALU.mult))
                        p.v(lambda e, dc=dc, cix=cix: e.scalar_tensor_tensor(
                            htr[:, dc * SG:(dc + 1) * SG], xabuf[:, dc * SG:(dc + 1) * SG],
                            skip_t[:, cix:cix + 1], htr[:, dc * SG:(dc + 1) * SG], ALU.mult, ALU.add))
                        p.v(lambda e, dc=dc: e.tensor_mul(
                            hstg[:, dc * SG:(dc + 1) * SG], htr[:, dc * SG:(dc + 1) * SG],
                            szbuf[:, dc * SG:(dc + 1) * SG]),
                            inc=("dve" if dc == 3 else None))
                    p.guard_set("htrbuf", "dve")
                    p.guard_set("hps", "dve")
                    p.wait("gpsimd", "dve")
                    for dc in range(4):
                        p.g(lambda e, hd=hd, dc=dc, s0=s0: e.dma_start(
                            out=SC["hst_s"][hd * 512 + dc * 128:hd * 512 + (dc + 1) * 128, s0:s0 + SG],
                            in_=hstg[:, dc * SG:(dc + 1) * SG]), inc="st")
                    p.guard_set("hstg", "gst")
                    p.guard_set("eibuf", "mm")
                    p.guard_set("xaszbuf", "dve")
                    p.guard_set("gsl", "act")
                p.guard_set("p2bufs", "mm")
            if debug:
                p.wait("gpsimd", "gst")
                p.g(lambda e: e.dma_start(out=dbg["d_hst"], in_=SC["hst_s"]), inc="st")
                p.wait("gpsimd", "gst")
            p.flush(blk)
        P2.close()

    if 2 in phases:
        P2b = ExitStack()
        wdnb = sb(P2b, "wdnb", (128, 16 * E), BF16)
        hstb = sb(P2b, "hstb", (128, 16 * TP1), BF16)
        xtf = sb(P2b, "xtf", (128, 8 * TP1))
        x2st = sb(P2b, "x2st", (128, 4 * TP1))
        ps_w0 = psum(P2b, "ps_w0", (128, 512))
        ps_w1 = psum(P2b, "ps_w1", (128, 512))
        with nc.Block() as blk:
            p.wait("sync", "gst", cnt["gst"])
            p.g(lambda e: e.dma_start(
                out=wdnb[:].rearrange("p (c n) -> p c n", c=16),
                in_=D["wdn"].rearrange("(c p) n -> p c n", p=128)), inc="gld")
            p.wait("tensor", "gld")
            for t in range(NT1):
                t0 = t * TP1
                p.guard_wait("sync", "p2bldh")
                p.sy(lambda e, t0=t0: e.dma_start(
                    out=hstb[:].rearrange("p (c n) -> p c n", c=16),
                    in_=SC["hst_s"].rearrange("(c p) n -> p c n", p=128)[:, :, t0:t0 + TP1]))
                p.sy(lambda e, t0=t0: e.dma_start(
                    out=xtf[:].rearrange("p (c n) -> p c n", c=8),
                    in_=D["xT"].rearrange("(c p) n -> p c n", p=128)[:, :, t0:t0 + TP1]))
                ld_t = cnt["ld"]
                p.wait("tensor", "ld", ld_t)
                for dc in range(8):
                    pb = ps_w0 if dc % 2 == 0 else ps_w1
                    p.guard_wait("tensor", f"wdb{dc % 2}")
                    for hc in range(16):
                        p.t(lambda e, dc=dc, hc=hc, pb=pb: e.matmul(
                            pb[:, 0:TP1], wdnb[:, hc * E + dc * 128:hc * E + (dc + 1) * 128],
                            hstb[:, hc * TP1:(hc + 1) * TP1],
                            start=(hc == 0), stop=(hc == 15)), inc=("mm" if hc == 15 else None))
                    mmv = cnt["mm"]
                    sl = dc % 4
                    p.wait("vector", "mm", mmv)
                    p.guard_wait("vector", f"x2st{sl}")
                    p.v(lambda e, dc=dc, pb=pb, sl=sl: e.scalar_tensor_tensor(
                        x2st[:, sl * TP1:(sl + 1) * TP1], pb[:, 0:TP1], modv[:, 16 + dc:17 + dc],
                        xtf[:, dc * TP1:(dc + 1) * TP1], ALU.mult, ALU.add), inc="dve")
                    p.guard_set(f"wdb{dc % 2}", "dve")
                    p.wait("gpsimd", "dve")
                    p.g(lambda e, dc=dc, t0=t0, sl=sl: e.dma_start(
                        out=SC["x2_s"][dc * 128:(dc + 1) * 128, t0:t0 + TP1],
                        in_=x2st[:, sl * TP1:(sl + 1) * TP1]), inc="st")
                    p.guard_set(f"x2st{sl}", "gst")
                p.guard_set("p2bldh", "mm")
            if debug:
                p.wait("gpsimd", "gst")
                p.g(lambda e: e.dma_start(out=dbg["d_x2"], in_=SC["x2_s"]), inc="st")
                p.wait("gpsimd", "gst")
            p.flush(blk)
        P2b.close()

    if 3 in phases:
        P3 = ExitStack()
        TW = TP3 + 2
        w1qb = sb(P3, "w1qb", (128, 8 * DHID), BF16)
        w2qb = sb(P3, "w2qb", (128, 32 * E), BF16)
        x2t = sb(P3, "x2t", (128, 8 * TW), BF16)
        x2res = sb(P3, "x2res", (128, 2 * TP3))
        sqa = sb(P3, "sqa", (128, TW))
        sqb = sb(P3, "sqb", (128, TW))
        rows3 = sb(P3, "rows3", (1, 3 * TW))
        h2t = sb(P3, "h2t", (128, 8 * TW), BF16)
        runm = sb(P3, "runm", (128, TW), BF16)
        runn = sb(P3, "runn", (128, TW), BF16)
        colsA = sb(P3, "colsA", (128, 72))
        colsB = sb(P3, "colsB", (128, 72))
        colsC = sb(P3, "colsC", (128, 72))
        colsD = sb(P3, "colsD", (128, 72))
        colsE = sb(P3, "colsE", (128, 72))
        scb = sb(P3, "scb", (128, TW), BF16)
        rfb = sb(P3, "rfb", (128, TW))
        xq1 = sb(P3, "xq1", (128, 8 * TW), BF16)
        h1t = sb(P3, "h1t", (128, 32 * TW), BF16)
        cacc3 = sb(P3, "cacc3", (128, TW))
        xst = sb(P3, "xst", (128, 32 * TP3), BF16)
        xq2 = sb(P3, "xq2", (128, 32 * TP3), BF16)
        outst = sb(P3, "outst", (128, TP3))
        yq1 = sqa
        ym1 = sb(P3, "ym1", (128, TW))
        ym2 = sqb
        ps_m0 = psum(P3, "ps_m0", (128, 512))
        ps_m1 = psum(P3, "ps_m1", (128, 512))
        ps_r3 = psum(P3, "ps_r3", (128, 512))
        ps_b3 = psum(P3, "ps_b3", (128, 512))
        ps_t3 = psum(P3, "ps_t3", (128, 512))

        def quant_cols(p, srccols, n, wi):
            pass

        with nc.Block() as blk:
            p.wait("sync", "gst", cnt["gst"])
            p.g(lambda e: e.dma_start(
                out=w1qb[:].rearrange("p (c n) -> p c n", c=8),
                in_=SC["w1q_s"].rearrange("(c p) n -> p c n", p=128)), inc="gld")
            p.g(lambda e: e.dma_start(
                out=w2qb[:].rearrange("p (c n) -> p c n", c=32),
                in_=SC["w2q_s"].rearrange("(c p) n -> p c n", p=128)), inc="gld")
            p.wait("tensor", "gld")
            for buf, wd in ((h2t, 8 * TW), (xq1, 8 * TW), (runm, TW), (runn, TW),
                            (colsA, 72), (colsB, 72), (colsC, 72), (colsD, 72), (colsE, 72)):
                p.v(lambda e, buf=buf, wd=wd: e.memset(buf[:, 0:wd], 0.0))
            p.v(lambda e: e.memset(rows3[:], 0.0), inc="dve")
            p.wait("vector", "dve")
            p.wait("tensor", "dve")
            p.wait("scalar", "dve")
            for t in range(NT3):
                t0 = t * TP3
                lo = 1 if t == 0 else 0
                hi = 1 if t == NT3 - 1 else 0
                a0 = t0 - 1 + lo
                a1 = t0 + TP3 + 1 - hi
                nw_ = a1 - a0
                off = lo
                p.guard_wait("gpsimd", "x2t")
                p.g(lambda e, a0=a0, nw_=nw_, off=off: e.dma_start(
                    out=x2t[:].rearrange("p (c n) -> p c n", c=8)[:, :, off:off + nw_],
                    in_=SC["x2_s"].rearrange("(c p) n -> p c n", p=128)[:, :, a0:a0 + nw_]), inc="gld")
                ld_t = cnt["gld"]
                p.wait("tensor", "gld", ld_t)
                p.guard_wait("tensor", "psr3")
                for c in range(8):
                    p.t(lambda e, c=c, off=off, nw_=nw_: e.matmul(
                        ps_r3[0:1, 0:nw_], onescol_b[:, 0:1], x2t[:, c * TW + off:c * TW + off + nw_],
                        start=(c == 0), stop=(c == 7)), inc=("mm" if c == 7 else None))
                p.wait("scalar", "gld", ld_t)
                for c in range(8):
                    d_ = sqa if c % 2 == 0 else sqb
                    if c >= 2:
                        p.wait("scalar", "mm", cnt["mm"])
                    p.s(lambda e, c=c, d=d_, off=off, nw_=nw_: e.activation(
                        d[:, off:off + nw_], x2t[:, c * TW + off:c * TW + off + nw_], AF.Square), inc="act")
                    p.wait("tensor", "act")
                    p.t(lambda e, c=c, d=d_, off=off, nw_=nw_: e.matmul(
                        ps_r3[32:33, 0:nw_], onescol[:, 0:1], d[:, off:off + nw_],
                        start=(c == 0), stop=(c == 7)), inc="mm")
                p.wait("vector", "mm")
                u0 = rows3[0:1, 0:TW]; u1 = rows3[0:1, TW:2 * TW]; u2 = rows3[0:1, 2 * TW:3 * TW]
                p.v(lambda e, u0=u0, nw_=nw_: e.tensor_scalar_mul(u0[0:1, 0:nw_], ps_r3[0:1, 0:nw_], 1.0 / E))
                p.v(lambda e, u1=u1, nw_=nw_: e.tensor_scalar_mul(u1[0:1, 0:nw_], ps_r3[32:33, 0:nw_], 1.0 / E), inc="dve")
                p.guard_set("psr3", "dve")
                p.wait("vector", "dve")
                p.v(lambda e, u0=u0, u2=u2, nw_=nw_: e.tensor_mul(u2[0:1, 0:nw_], u0[0:1, 0:nw_], u0[0:1, 0:nw_]), inc="dve")
                p.wait("vector", "dve")
                p.v(lambda e, u1=u1, u2=u2, nw_=nw_: e.tensor_sub(u1[0:1, 0:nw_], u1[0:1, 0:nw_], u2[0:1, 0:nw_]), inc="dve")
                p.wait("vector", "dve")
                p.v(lambda e, u1=u1, nw_=nw_: e.tensor_scalar_add(u1[0:1, 0:nw_], u1[0:1, 0:nw_], 1e-5), inc="dve")
                p.wait("scalar", "dve")
                p.s(lambda e, u1=u1, u2=u2, nw_=nw_: e.activation(u2[0:1, 0:nw_], u1[0:1, 0:nw_], AF.Sqrt), inc="act")
                p.wait("vector", "act")
                p.v(lambda e, u1=u1, u2=u2, nw_=nw_: e.reciprocal(u1[0:1, 0:nw_], u2[0:1, 0:nw_]), inc="dve")
                p.wait("vector", "dve")
                p.v(lambda e, u0=u0, u1=u1, u2=u2, nw_=nw_: e.tensor_mul(u2[0:1, 0:nw_], u0[0:1, 0:nw_], u1[0:1, 0:nw_]), inc="dve")
                p.wait("vector", "dve")
                p.v(lambda e, u2=u2, nw_=nw_: e.tensor_scalar_mul(u2[0:1, 0:nw_], u2[0:1, 0:nw_], -1.0), inc="dve")
                p.wait("tensor", "dve")
                p.guard_wait("tensor", "psb3")
                p.t(lambda e, u1=u1, nw_=nw_: e.matmul(ps_b3[:, 0:nw_], onesrow[:], u1[0:1, 0:nw_], start=True, stop=True))
                p.t(lambda e, u2=u2, nw_=nw_: e.matmul(ps_t3[:, 0:nw_], onesrow[:], u2[0:1, 0:nw_], start=True, stop=True), inc="mm")
                p.wait("vector", "mm")
                p.guard_wait("vector", "h2t")
                for c in range(8):
                    p.v(lambda e, c=c, off=off, nw_=nw_: e.tensor_mul(
                        sqa[:, off:off + nw_], x2t[:, c * TW + off:c * TW + off + nw_], ps_b3[:, 0:nw_]))
                    p.v(lambda e, nw_=nw_, off=off: e.tensor_add(
                        sqa[:, off:off + nw_], sqa[:, off:off + nw_], ps_t3[:, 0:nw_]))
                    p.v(lambda e, c=c, off=off, nw_=nw_: e.tensor_scalar(
                        h2t[:, c * TW + off:c * TW + off + nw_], sqa[:, off:off + nw_],
                        modv[:, 32 + c:33 + c], modv[:, 24 + c:25 + c], ALU.mult, ALU.add),
                        inc=("dve" if c == 7 else None))
                p.guard_set("psb3", "dve")
                h2_done = cnt["dve"]
                p.wait("vector", "dve", h2_done)
                p.guard_wait("vector", "runm")
                p.v(lambda e, off=off, nw_=nw_: e.tensor_copy(runm[:, off:off + nw_], h2t[:, off:off + nw_]))
                p.v(lambda e, off=off, nw_=nw_: e.tensor_copy(runn[:, off:off + nw_], h2t[:, off:off + nw_]), inc="dve")
                for c in range(1, 8):
                    p.wait("vector", "dve")
                    p.v(lambda e, c=c, off=off, nw_=nw_: e.tensor_tensor(
                        runm[:, off:off + nw_], runm[:, off:off + nw_],
                        h2t[:, c * TW + off:c * TW + off + nw_], ALU.max))
                    p.v(lambda e, c=c, off=off, nw_=nw_: e.tensor_tensor(
                        runn[:, off:off + nw_], runn[:, off:off + nw_],
                        h2t[:, c * TW + off:c * TW + off + nw_], ALU.min), inc="dve")
                p.wait("vector", "dve")
                p.v(lambda e, off=off, nw_=nw_: e.tensor_scalar_mul(runn[:, off:off + nw_], runn[:, off:off + nw_], -1.0), inc="dve")
                p.wait("vector", "dve")
                p.v(lambda e, off=off, nw_=nw_: e.tensor_max(runm[:, off:off + nw_], runm[:, off:off + nw_], runn[:, off:off + nw_]), inc="dve")
                p.wait("tensor", "dve")

                def colquant(wi, src_bf, dst_s29_col, dst_rf_col, nchunks):
                    p.guard_wait("tensor", "pst3q")
                    nb = (TW + 127) // 128
                    for b_ in range(nb):
                        w_ = min(128, TW - b_ * 128)
                        p.t(lambda e, b_=b_, w_=w_, src_bf=src_bf: e.matmul(
                            ps_t3[0:w_, b_ * 128:b_ * 128 + 128],
                            src_bf[:, b_ * 128:b_ * 128 + w_], identb[:],
                            start=True, stop=True), inc=("mm" if b_ == nb - 1 else None))
                    p.wait("vector", "mm")
                    for b_ in range(nb):
                        w_ = min(128, TW - b_ * 128)
                        p.v(lambda e, b_=b_, w_=w_: e.reduce_max(
                            colsA[0:w_, 32 * b_:32 * b_ + 1],
                            ps_t3[0:w_, b_ * 128:b_ * 128 + 128], axis=AX.X), inc=("dve" if b_ == nb - 1 else None))
                    p.guard_set("pst3q", "dve")
                    p.wait("vector", "dve")
                    p.v(lambda e: e.tensor_scalar_max(colsB[:, 0:72], colsA[:, 0:72], 1e-5), inc="dve")
                    p.wait("vector", "dve")
                    p.v(lambda e: e.reciprocal(colsC[:, 0:72], colsB[:, 0:72]), inc="dve")
                    p.wait("vector", "dve")
                    p.v(lambda e: e.tensor_scalar_mul(colsD[:, 0:72], colsC[:, 0:72], 127.0 * S29))
                    p.v(lambda e, wi=wi: e.tensor_scalar(
                        colsE[:, 0:72], colsB[:, 0:72], mwcol[:, wi:wi + 1], P29 / 127.0,
                        ALU.mult, ALU.mult), inc="dve")
                    p.wait("tensor", "dve")
                    for which, ct in ((0, colsD), (1, colsE)):
                        p.t(lambda e, ct=ct: e.matmul(
                            ps_t3[0:65, 0:128], ct[:, 0:65], ident[:, 0:128], start=True, stop=True), inc="mm")
                        p.wait("vector", "mm")
                        for b_ in range(nb):
                            w_ = min(128, TW - b_ * 128)
                            ro_off = 0 if which == 0 else TW
                            p.v(lambda e, b_=b_, w_=w_, ro_off=ro_off: e.tensor_copy(
                                rows3[0:1, ro_off + b_ * 128:ro_off + b_ * 128 + w_],
                                ps_t3[32 * b_:32 * b_ + 1, 0:w_]), inc=("dve" if b_ == nb - 1 else None))
                        p.wait("tensor", "dve")
                    p.t(lambda e: e.matmul(ps_b3[:, 0:TW], onesrow[:], rows3[0:1, 0:TW], start=True, stop=True), inc="mm")
                    p.wait("vector", "mm")
                    p.guard_wait("vector", "scbbuf")
                    p.v(lambda e: e.tensor_copy(scb[:], ps_b3[:, 0:TW]), inc="dve")
                    p.wait("tensor", "dve")
                    p.t(lambda e: e.matmul(ps_b3[:, 0:TW], onesrow[:], rows3[0:1, TW:2 * TW], start=True, stop=True), inc="mm")
                    p.wait("vector", "mm")
                    p.guard_wait("vector", "rfbbuf")
                    p.v(lambda e: e.tensor_copy(rfb[:], ps_b3[:, 0:TW]), inc="dve")

                colquant(0, runm, 0, 8, 8)
                p.wait("vector", "dve")
                for c in range(8):
                    p.v(lambda e, c=c: e.tensor_mul(yq1[:], h2t[:, c * TW:(c + 1) * TW], scb[:]))
                    p.v(lambda e: e.tensor_scalar(ym1[:], yq1[:], CP, CP, ALU.add, ALU.subtract))
                    p.v(lambda e: e.tensor_scalar(ym2[:], yq1[:], C23, C23, ALU.add, ALU.subtract))
                    p.v(lambda e, c=c: e.scalar_tensor_tensor(
                        xq1[:, c * TW:(c + 1) * TW], ym1[:], omi[:, 0:1], ym2[:], ALU.mult, ALU.add),
                        inc=("dve" if c == 7 else None))
                xq_done = cnt["dve"]
                p.guard_set("scbbuf", "dve")
                p.wait("tensor", "dve", xq_done)
                for hc in range(32):
                    pb = ps_m0 if hc % 2 == 0 else ps_m1
                    p.guard_wait("tensor", f"m3b{hc % 2}")
                    for kc in range(8):
                        p.t(lambda e, hc=hc, kc=kc, pb=pb: e.matmul(
                            pb[:, 0:TW], w1qb[:, kc * DHID + hc * 128:kc * DHID + (hc + 1) * 128],
                            xq1[:, kc * TW:(kc + 1) * TW],
                            start=(kc == 0), stop=(kc == 7)), inc=("mm" if kc == 7 else None))
                    mmv = cnt["mm"]
                    p.wait("vector", "mm", mmv)
                    if hc == 0:
                        p.guard_wait("vector", "h1t")
                    p.v(lambda e, hc=hc, pb=pb: e.tensor_mul(
                        h1t[:, hc * TW:(hc + 1) * TW], pb[:, 0:TW], rfb[:]), inc="dve")
                    p.guard_set(f"m3b{hc % 2}", "dve")
                h1_done = cnt["dve"]
                p.guard_set("rfbbuf", "dve")
                if t == 0 or t == NT3 - 1:
                    p.wait("vector", "dve", h1_done)
                    for c in range(32):
                        col = c * TW + (0 if t == 0 else TW - 1)
                        p.v(lambda e, col=col: e.tensor_scalar_mul(
                            h1t[:, col:col + 1], onescol[:, 0:1], 0.0), inc=("dve" if c == 31 else None))
                p.wait("vector", "dve")
                p.guard_wait("vector", "xstbuf")
                for c in range(32):
                    b_ = c * TW
                    cc_ = cacc3 if c % 2 == 0 else ym1
                    p.guard_wait("vector", f"cacc3{c % 2}")
                    p.v(lambda e, c=c, b_=b_, cc_=cc_: e.tensor_scalar(
                        cc_[:, 0:TP3], h1t[:, b_:b_ + TP3], conv2w[:, 0 * 32 + c:0 * 32 + c + 1], None, ALU.mult))
                    for tp_ in range(1, 3):
                        p.v(lambda e, c=c, b_=b_, tp_=tp_, cc_=cc_: e.scalar_tensor_tensor(
                            cc_[:, 0:TP3], h1t[:, b_ + tp_:b_ + tp_ + TP3], conv2w[:, tp_ * 32 + c:tp_ * 32 + c + 1],
                            cc_[:, 0:TP3], ALU.mult, ALU.add), inc=("dve" if tp_ == 2 else None))
                    p.wait("scalar", "dve")
                    p.s(lambda e, c=c, cc_=cc_: e.activation(
                        xst[:, c * TP3:(c + 1) * TP3], cc_[:, 0:TP3], AF.Silu,
                        bias=conv2b[:, c:c + 1], scale=1.0), inc="act")
                    p.guard_set(f"cacc3{c % 2}", "act")
                    p.wait("vector", "act")
                    if c == 0:
                        p.v(lambda e: e.tensor_copy(runm[:, 0:TP3], xst[:, 0:TP3]))
                        p.v(lambda e: e.tensor_copy(runn[:, 0:TP3], xst[:, 0:TP3]), inc="dve")
                    else:
                        p.v(lambda e, c=c: e.tensor_tensor(
                            runm[:, 0:TP3], runm[:, 0:TP3], xst[:, c * TP3:(c + 1) * TP3], ALU.max))
                        p.v(lambda e, c=c: e.tensor_tensor(
                            runn[:, 0:TP3], runn[:, 0:TP3], xst[:, c * TP3:(c + 1) * TP3], ALU.min), inc="dve")
                p.wait("vector", "dve")
                p.v(lambda e: e.tensor_scalar_mul(runn[:, 0:TP3], runn[:, 0:TP3], -1.0), inc="dve")
                p.wait("vector", "dve")
                p.v(lambda e: e.tensor_max(runm[:, 0:TP3], runm[:, 0:TP3], runn[:, 0:TP3]), inc="dve")
                p.wait("tensor", "dve")
                colquant(1, runm, 0, 8, 32)
                p.wait("vector", "dve")
                for c in range(32):
                    p.v(lambda e, c=c: e.tensor_mul(yq1[:, 0:TP3], xst[:, c * TP3:(c + 1) * TP3], scb[:, 0:TP3]))
                    p.v(lambda e: e.tensor_scalar(ym1[:, 0:TP3], yq1[:, 0:TP3], CP, CP, ALU.add, ALU.subtract))
                    p.v(lambda e: e.tensor_scalar(ym2[:, 0:TP3], yq1[:, 0:TP3], C23, C23, ALU.add, ALU.subtract))
                    p.v(lambda e, c=c: e.scalar_tensor_tensor(
                        xq2[:, c * TP3:(c + 1) * TP3], ym1[:, 0:TP3], omi[:, 0:1], ym2[:, 0:TP3], ALU.mult, ALU.add),
                        inc=("dve" if c == 31 else None))
                xq2_done = cnt["dve"]
                p.guard_set("scbbuf", "dve")
                p.guard_set("xstbuf", "dve")
                p.wait("tensor", "dve", xq2_done)
                for dc in range(8):
                    p.guard_wait("sync", f"x2res{dc % 2}")
                    p.sy(lambda e, dc=dc, t0=t0: e.dma_start(
                        out=x2res[:, (dc % 2) * TP3:(dc % 2 + 1) * TP3],
                        in_=SC["x2_s"][dc * 128:(dc + 1) * 128, t0:t0 + TP3]))
                    x2r_ld = cnt["ld"]
                    pb = ps_m0 if dc % 2 == 0 else ps_m1
                    p.guard_wait("tensor", f"m3b{dc % 2}")
                    for kc in range(32):
                        p.t(lambda e, dc=dc, kc=kc, pb=pb: e.matmul(
                            pb[:, 0:TP3], w2qb[:, kc * E + dc * 128:kc * E + (dc + 1) * 128],
                            xq2[:, kc * TP3:(kc + 1) * TP3],
                            start=(kc == 0), stop=(kc == 31)), inc=("mm" if kc == 31 else None))
                    mmv = cnt["mm"]
                    sl = 0
                    p.wait("vector", "mm", mmv)
                    p.guard_wait("vector", f"outst{sl}")
                    p.v(lambda e, dc=dc, pb=pb: e.tensor_mul(ym1[:, 0:TP3], pb[:, 0:TP3], rfb[:, 0:TP3]), inc="dve")
                    p.guard_set(f"m3b{dc % 2}", "dve")
                    p.wait("vector", "dve")
                    p.wait("vector", "ld", x2r_ld)
                    p.v(lambda e, dc=dc, sl=sl: e.scalar_tensor_tensor(
                        outst[:, sl * TP3:(sl + 1) * TP3], ym1[:, 0:TP3], modv[:, 40 + dc:41 + dc],
                        x2res[:, (dc % 2) * TP3:(dc % 2 + 1) * TP3], ALU.mult, ALU.add), inc="dve")
                    p.guard_set(f"x2res{dc % 2}", "dve")
                    p.wait("gpsimd", "dve")
                    p.g(lambda e, dc=dc, t0=t0, sl=sl: e.dma_start(
                        out=yT[dc * 128:(dc + 1) * 128, t0:t0 + TP3],
                        in_=outst[:, sl * TP3:(sl + 1) * TP3]), inc="st")
                    p.guard_set(f"outst{sl}", "gst")
                p.guard_set("rfbbuf", "dve")
                p.guard_set("h1t", "mm")
                p.guard_set("h2t", "mm")
                p.guard_set("runm", "dve")
                p.guard_set("x2t", "dve")
            p.wait("gpsimd", "gst")
            p.wait("sync", "st", cnt["st"])
            p.flush(blk)
        P3.close()

    A.close()
    top.close()
    return nc, dbg, SC, D


_BUILD_CACHE = {}


def kernel(**inputs):
    maps = prep_inputs(inputs)
    in_maps = [{("i_" + k): v for k, v in m.items()} for m in maps]
    key = ("full", DEBUG)
    if key not in _BUILD_CACHE:
        _BUILD_CACHE[key] = build(debug=DEBUG)
    nc, dbg, SC, D = _BUILD_CACHE[key]
    res = run_bass_kernel_spmd(nc, in_maps, core_ids=list(range(B)))
    out = np.stack([np.ascontiguousarray(r["yT"].T) for r in res.results], axis=0)
    return out.astype(np.float32)
